# version 03
# baseline (speedup 1.0000x reference)
import contextlib
import math
import numpy as np
import ml_dtypes
import concourse.bass as bass
import concourse.mybir as mybir
from concourse.bass_utils import run_bass_kernel_spmd

F32 = mybir.dt.float32
BF16 = mybir.dt.bfloat16
AF = mybir.ActivationFunctionType
ALU = mybir.AluOpType
AXX = mybir.AxisListType.X

NC = 8
T = 2048
D = 1024
NT = T // 128
L = 8192
EPS = 1e-5


class Prog:

    def __init__(self, name="prog"):
        self.nc = bass.Bass("TRN2", target_bir_lowering=False, num_devices=NC, name=name,
                            detect_race_conditions=False)
        self.es = contextlib.ExitStack()
        self._scopes = []
        self._n = 0

    def uid(self):
        self._n += 1
        return self._n

    def sb(self, name, shape, dt=F32):
        return self._cur().enter_context(self.nc.sbuf_tensor(name, shape, dt))

    def ps(self, name, shape, dt=F32):
        return self._cur().enter_context(self.nc.psum_tensor(name, shape, dt))

    def sem(self, name):
        return self.es.enter_context(self.nc.semaphore(name))

    def _cur(self):
        return self._scopes[-1] if self._scopes else self.es

    def begin(self):
        self._scopes.append(contextlib.ExitStack())

    def end(self):
        self._scopes.pop().close()

    def din(self, name, shape, dt=F32):
        return self.nc.dram_tensor(name, shape, dt, kind="ExternalInput").ap()

    def dout(self, name, shape, dt=F32):
        return self.nc.dram_tensor(name, shape, dt, kind="ExternalOutput").ap()

    def dint(self, name, shape, dt=F32):
        return self.nc.dram_tensor(name, shape, dt, kind="Internal").ap()

    def close(self):
        self.es.close()

    def cc(self, val):
        return self._ccmap[val]


def v3(ap, dc=8):
    return ap.rearrange("p (dc t) -> p dc t", dc=dc)


def emit_consts(p, ccvals=()):
    c = {}
    c["ones"] = p.sb("c_ones", [128, 128], F32)
    c["ones_bf"] = p.sb("c_ones_bf", [128, 128], BF16)
    c["ident_bf"] = p.sb("c_ident_bf", [128, 128], BF16)
    c["ident"] = p.sb("c_ident", [128, 128], F32)
    ccvals = sorted(set(ccvals))
    cctile = p.sb("c_cc", [128, max(1, len(ccvals))], F32)
    p._ccmap = {v: cctile[:, j:j + 1] for j, v in enumerate(ccvals)}
    c["LT"] = p.sb("c_LT", [128, 128], F32)
    c["e127"] = p.sb("c_e127", [128, 1], F32)
    s_c = p.sem("c_sem")
    with p.nc.Block() as block:
        @block.vector
        def _(v):
            v.memset(c["ones"][:], 1.0).then_inc(s_c, 1)
            v.memset(c["ones_bf"][:], 1.0).then_inc(s_c, 1)

        @block.gpsimd
        def _(g):
            g.wait_ge(s_c, 2)
            g.affine_select(c["ident"][:], c["ones"][:], [[-1, 128]], ALU.is_equal, 0.0,
                            base=0, channel_multiplier=1)
            g.affine_select(c["ident_bf"][:], c["ones_bf"][:], [[-1, 128]], ALU.is_equal, 0.0,
                            base=0, channel_multiplier=1)
            g.affine_select(c["LT"][:], c["ones"][:], [[1, 128]], ALU.is_ge, 0.0,
                            base=0, channel_multiplier=-1)
            g.affine_select(c["e127"][:], c["ones"][:, 0:1], [[1, 1]], ALU.is_equal, 0.0,
                            base=-127, channel_multiplier=1)
            for j, v in enumerate(ccvals):
                g.memset(cctile[:, j:j + 1], v)
    return c


def emit_ln_xnT(p, c, x_dram, xnT, nt=NT, xres=None):
    u = p.uid()
    Tn = nt * 128
    xa = p.sb(f"ln_xa_{u}", [128, 2 * 1024], F32)
    xn = p.sb(f"ln_xn_{u}", [128, 2 * 1024], BF16)
    sq = p.sb(f"ln_sq_{u}", [128, 2 * 1024], F32)
    cols = p.sb(f"ln_cols_{u}", [128, 12], F32)
    pst = p.ps(f"ln_ps_{u}", [128, 2048], BF16)
    s_ld = [p.sem(f"ln_ld{j}_{u}") for j in range(2)]
    s_sq = p.sem(f"ln_sq_{u}")
    s_stat = p.sem(f"ln_st_{u}")
    s_rstd = p.sem(f"ln_rs_{u}")
    s_xn = p.sem(f"ln_xn_{u}")
    s_tr = p.sem(f"ln_tr_{u}")
    s_cp = p.sem(f"ln_cp_{u}")

    with p.nc.Block() as block:
        @block.sync
        def _(sync):
            for i in range(nt):
                if i >= 2:
                    sync.wait_ge(s_xn, i - 1)
                b = (i % 2) * 1024
                sync.dma_start(out=xa[:, b:b + 1024],
                               in_=x_dram[i * 128:(i + 1) * 128, :]).then_inc(s_ld[i % 2], 16)

        @block.gpsimd
        def _(g):
            for i in range(nt):
                b = (i % 2) * 1024
                g.wait_ge(s_ld[i % 2], 16 * (i // 2 + 1))
                g.tensor_tensor(sq[:, b:b + 1024], xa[:, b:b + 1024], xa[:, b:b + 1024],
                                ALU.mult).then_inc(s_sq, 1)

        @block.vector
        def _(v):
            for i in range(nt):
                b = (i % 2) * 1024
                cb = (i % 2) * 6
                v.wait_ge(s_ld[i % 2], 16 * (i // 2 + 1))
                if i >= 2:
                    v.wait_ge(s_rstd, i - 1)
                v.tensor_reduce(cols[:, cb:cb + 1], xa[:, b:b + 1024], AXX, ALU.add)
                v.wait_ge(s_sq, i + 1)
                v.tensor_reduce(cols[:, cb + 1:cb + 2], sq[:, b:b + 1024], AXX, ALU.add)
                v.drain()
                v.tensor_scalar(cols[:, cb + 2:cb + 3], cols[:, cb:cb + 1], 1.0 / D, None, ALU.mult)
                v.drain()
                v.tensor_tensor(cols[:, cb + 3:cb + 4], cols[:, cb + 2:cb + 3],
                                cols[:, cb + 2:cb + 3], ALU.mult)
                v.drain()
                v.scalar_tensor_tensor(cols[:, cb + 4:cb + 5], cols[:, cb + 1:cb + 2], 1.0 / D,
                                       cols[:, cb + 3:cb + 4], ALU.mult, ALU.subtract).then_inc(s_stat, 1)
                if xres is not None:
                    v.tensor_copy(xres[:, i * 1024:(i + 1) * 1024], xa[:, b:b + 1024])
                v.wait_ge(s_rstd, i + 1)
                if i >= 2:
                    v.wait_ge(s_tr, i - 1)
                v.tensor_scalar(xn[:, b:b + 1024], xa[:, b:b + 1024],
                                cols[:, cb + 2:cb + 3], cols[:, cb + 5:cb + 6],
                                ALU.subtract, ALU.mult).then_inc(s_xn, 1)
                if i >= 1:
                    v.wait_ge(s_tr, i)
                    v.tensor_copy(v3(xnT[:])[:, :, (i - 1) * 128:i * 128],
                                  v3(pst[:, ((i - 1) % 2) * 1024:((i - 1) % 2) * 1024 + 1024])
                                  ).then_inc(s_cp, 1)
            v.wait_ge(s_tr, nt)
            v.tensor_copy(v3(xnT[:])[:, :, (nt - 1) * 128:nt * 128],
                          v3(pst[:, ((nt - 1) % 2) * 1024:((nt - 1) % 2) * 1024 + 1024])
                          ).then_inc(s_cp, 1)

        @block.scalar
        def _(s):
            for i in range(nt):
                cb = (i % 2) * 6
                s.wait_ge(s_stat, i + 1)
                s.activation(cols[:, cb + 5:cb + 6], cols[:, cb + 4:cb + 5], AF.Ln,
                             bias=p.cc(EPS), scale=1.0)
                s.drain()
                s.activation(cols[:, cb + 5:cb + 6], cols[:, cb + 5:cb + 6], AF.Exp,
                             bias=0.0, scale=p.cc(-0.5)).then_inc(s_rstd, 1)

        @block.tensor
        def _(t):
            for i in range(nt):
                b = (i % 2) * 1024
                pb = (i % 2) * 1024
                t.wait_ge(s_xn, i + 1)
                if i >= 2:
                    t.wait_ge(s_cp, i - 1)
                for dc in range(8):
                    last = t.transpose(pst[:, pb + dc * 128:pb + (dc + 1) * 128],
                                       xn[:, b + dc * 128:b + (dc + 1) * 128], c["ident_bf"][:])
                last.then_inc(s_tr, 1)

    return


def emit_bias_cols(p, c, b_dram_row, out_sb, n):
    u = p.uid()
    p.begin()
    CH = 8
    row = p.sb(f"bc_row_{u}", [1, 1024], F32)
    psb = p.ps(f"bc_ps_{u}", [128, 512], F32)
    s_l = p.sem(f"bc_l_{u}")
    s_m = p.sem(f"bc_m_{u}")
    s_v = p.sem(f"bc_v_{u}")
    nch = (n + CH - 1) // CH
    with p.nc.Block() as block:
        @block.sync
        def _(sync):
            for k in range(nch):
                if k >= 1:
                    sync.wait_ge(s_m, k)
                j0 = k * CH
                j1 = min(n, j0 + CH)
                sync.dma_start(out=row[0:1, 0:(j1 - j0) * 128],
                               in_=b_dram_row[0:1, j0 * 128:j1 * 128]).then_inc(s_l, 16)

        @block.tensor
        def _(t):
            for k in range(nch):
                t.wait_ge(s_l, 16 * (k + 1))
                if k >= 1:
                    t.wait_ge(s_v, k)
                j0 = k * CH
                j1 = min(n, j0 + CH)
                for j in range(j1 - j0):
                    last = t.matmul(psb[:, j:j + 1], row[0:1, j * 128:(j + 1) * 128],
                                    c["ones"][0:1, 0:1], start=True, stop=True)
                last.then_inc(s_m, 1)

        @block.vector
        def _(v):
            for k in range(nch):
                j0 = k * CH
                j1 = min(n, j0 + CH)
                v.wait_ge(s_m, k + 1)
                v.tensor_copy(out_sb[:, j0:j1], psb[:, 0:j1 - j0]).then_inc(s_v, 1)
    p.end()
    return


def emit_qkv(p, c, xnT, w_dram, b_dram, qT_sb, kT_sb, v_sb, qss_sb, kmax_sb):
    u = p.uid()
    w_sb = p.sb(f"qkv_w_{u}", [128, 8 * 3072], BF16)
    b_sb = p.sb(f"qkv_b_{u}", [128, 16], F32)
    bv_row = p.sb(f"qkv_bv_{u}", [1, 1024], F32)
    sqs = p.sb(f"qkv_sq_{u}", [128, 1024], BF16)
    kssrow = p.sb(f"qkv_kss_{u}", [1, 2048], F32)
    emit_bias_cols(p, c, b_dram[0:2048].rearrange("(a o) -> a o", a=1), b_sb, 16)
    ps = [p.ps(f"qkv_ps{j}_{u}", [128, 512], F32) for j in range(8)]
    psn = ps[7]
    s_w = [p.sem(f"qkv_w{j}_{u}") for j in range(2)]
    s_mm = p.sem(f"qkv_mm_{u}")
    s_ev = p.sem(f"qkv_ev_{u}")
    s_sq = p.sem(f"qkv_sqs_{u}")
    s_nm = p.sem(f"qkv_nm_{u}")

    wr = w_dram.rearrange("(dc p) o -> p dc o", p=128)
    w3 = v3(w_sb[:])

    with p.nc.Block() as block:
        @block.gpsimd
        def _(g):
            for dc in range(8):
                g.dma_start(out=w3[:, dc, :], in_=wr[:, dc, :]).then_inc(s_w[0], 16)

        @block.sync
        def _(sync):
            sync.dma_start(out=bv_row[:], in_=b_dram[2048:3072].rearrange("(a o) -> a o", a=1)
                           ).then_inc(s_w[1], 16)

    with p.nc.Block() as block:
        @block.tensor
        def _(t):
            t.wait_ge(s_w[0], 16 * 8)
            nmm = 0
            for oc in range(16):
                pbase = 4 * (oc % 2)
                if oc >= 2:
                    t.wait_ge(s_ev, (oc - 1) * 4)
                for dc in range(8):
                    for tt in range(4):
                        last = t.matmul(ps[pbase + tt][:],
                                        w3[:, dc, oc * 128:(oc + 1) * 128],
                                        v3(xnT[:])[:, dc, tt * 512:(tt + 1) * 512],
                                        start=(dc == 0), stop=(dc == 7))
                last.then_inc(s_mm, 1)

        @block.scalar
        def _(s):
            for oc in range(16):
                pbase = 4 * (oc % 2)
                dst = v3(qT_sb[:]) if oc < 8 else v3(kT_sb[:])
                dcol = oc if oc < 8 else oc - 8
                s.wait_ge(s_mm, oc + 1)
                for tt in range(4):
                    s.activation(dst[:, dcol, tt * 512:(tt + 1) * 512], ps[pbase + tt][:],
                                 AF.Identity, bias=b_sb[:, oc:oc + 1], scale=1.0).then_inc(s_ev, 1)

    with p.nc.Block() as block:
        @block.tensor
        def _(t):
            for tt in range(16):
                pbase = 2 * (tt % 2)
                if tt >= 2:
                    t.wait_ge(s_ev, 64 + (tt - 1) * 2)
                for dc in range(8):
                    for dvh in range(2):
                        t.matmul(ps[pbase + dvh][:],
                                 v3(xnT[:])[:, dc, tt * 128:(tt + 1) * 128],
                                 w3[:, dc, 2048 + dvh * 512:2048 + (dvh + 1) * 512],
                                 start=(dc == 0), stop=False)
                for dvh in range(2):
                    last = t.matmul(ps[pbase + dvh][:], c["ones"][0:1, 0:128],
                                    bv_row[0:1, dvh * 512:(dvh + 1) * 512],
                                    start=False, stop=True)
                last.then_inc(s_mm, 1)

        @block.vector
        def _(v):
            for tt in range(16):
                pbase = 2 * (tt % 2)
                v.wait_ge(s_mm, 16 + tt + 1)
                for dvh in range(2):
                    v.tensor_copy(v3(v_sb[:], dc=16)[:, tt, dvh * 512:(dvh + 1) * 512],
                                  ps[pbase + dvh][:]).then_inc(s_ev, 1)

    s_nmm = p.sem(f"qkv_nmm_{u}")
    s_ncp = p.sem(f"qkv_ncp_{u}")
    with p.nc.Block() as block:
        @block.scalar
        def _(sc):
            for half in range(2):
                src = v3(qT_sb[:]) if half == 0 else v3(kT_sb[:])
                for tt in range(4):
                    for dc in range(8):
                        q = half * 32 + tt * 8 + dc
                        sb2 = (q % 2) * 512
                        if q >= 2:
                            sc.wait_ge(s_nmm, q - 1)
                        sc.activation(sqs[:, sb2:sb2 + 512], src[:, dc, tt * 512:(tt + 1) * 512],
                                      AF.Square, bias=0.0, scale=1.0).then_inc(s_sq, 1)

        @block.tensor
        def _(t):
            for half in range(2):
                for tt in range(4):
                    g8 = half * 4 + tt
                    if g8 >= 1:
                        t.wait_ge(s_ncp, g8)
                    for dc in range(8):
                        q = half * 32 + tt * 8 + dc
                        sb2 = (q % 2) * 512
                        t.wait_ge(s_sq, q + 1)
                        t.matmul(psn[0:1, 0:512], c["ones_bf"][:, 0:1],
                                 sqs[:, sb2:sb2 + 512],
                                 start=(dc == 0), stop=(dc == 7)).then_inc(s_nmm, 1)

        @block.vector
        def _(v):
            for half in range(2):
                dst = qss_sb if half == 0 else kssrow
                for tt in range(4):
                    g8 = half * 4 + tt
                    v.wait_ge(s_nmm, g8 * 8 + 8)
                    v.tensor_copy(dst[0:1, tt * 512:(tt + 1) * 512],
                                  psn[0:1, 0:512]).then_inc(s_ncp, 1)

    with p.nc.Block() as block:
        @block.vector
        def _(v):
            v.tensor_reduce(kmax_sb[0:1, 0:1], kssrow[0:1, 0:2048], AXX, ALU.max)
    return


def emit_row_to_cols(p, c, row_sb_ap, out_sb_ap, n, psbank):
    u = p.uid()
    s_m = p.sem(f"r2c_{u}")
    with p.nc.Block() as block:
        @block.tensor
        def _(t):
            for j in range(n):
                last = t.matmul(psbank[:, j:j + 1], row_sb_ap[0:1, j * 128:(j + 1) * 128],
                                c["ones"][0:1, 0:1], start=True, stop=True)
            last.then_inc(s_m, 1)

        @block.vector
        def _(v):
            v.wait_ge(s_m, 1)
            v.tensor_copy(out_sb_ap[:, 0:n], psbank[:, 0:n])
    return


def emit_attn(p, c, core, qT_sb, kT_dram, v_dram, qss_dram, kss4_dram, gamma_dram,
              wp_dram, bp_dram, x_dram, xout_dram):
    u = p.uid()
    qoff = (core % 4) * NT

    O_sb = p.sb(f"at_O_{u}", [128, 16 * 1024], BF16)
    OT = p.sb(f"at_OT_{u}", [128, 8 * 2048], BF16)
    wp = p.sb(f"at_wp_{u}", [128, 8 * 1024], BF16)
    bp_row = p.sb(f"at_bp_{u}", [1, 1024], F32)
    kt_b = [p.sb(f"at_kt{j}_{u}", [128, 8 * 512], BF16) for j in range(2)]
    vt_b = [p.sb(f"at_vt{j}_{u}", [128, 4 * 1024], BF16) for j in range(2)]
    Pt = p.sb(f"at_pt_{u}", [128, 512], BF16)
    PtT = p.sb(f"at_ptt_{u}", [128, 512], BF16)
    zacc = p.sb(f"at_z_{u}", [128, 52], F32)
    nmq = p.sb(f"at_nmq_{u}", [128, 16], F32)
    decay = p.sb(f"at_dk_{u}", [128, 128], F32)
    rows = p.sb(f"at_rows_{u}", [1, 2048], F32)
    srow = p.sb(f"at_srow_{u}", [1, 8], F32)
    dmat = p.sb(f"at_dm_{u}", [128, 128], F32)
    xa = p.sb(f"at_xa_{u}", [128, 2 * 1024], F32)
    xo = p.sb(f"at_xo_{u}", [128, 2 * 1024], F32)
    psO = [p.ps(f"at_psO{j}_{u}", [128, 512], F32) for j in range(6)]
    psS = p.ps(f"at_psS_{u}", [128, 512], F32)
    psT = p.ps(f"at_psT_{u}", [128, 1024], BF16)

    s_pre = p.sem(f"at_pre_{u}")
    s_prw = p.sem(f"at_prw_{u}")
    s_prb = p.sem(f"at_prb_{u}")
    s_kv = [p.sem(f"at_kv{j}_{u}") for j in range(2)]
    s_sg = p.sem(f"at_sg_{u}")
    s_msk = p.sem(f"at_msk_{u}")
    s_exp = p.sem(f"at_exp_{u}")
    s_tp = p.sem(f"at_tp_{u}")
    s_ptc = p.sem(f"at_ptc_{u}")
    s_ktd = p.sem(f"at_ktd_{u}")
    s_zr = p.sem(f"at_zr_{u}")
    s_oev = p.sem(f"at_oev_{u}")
    s_otr = p.sem(f"at_otr_{u}")
    s_occ = p.sem(f"at_occ_{u}")
    s_pmm = p.sem(f"at_pmm_{u}")
    s_xo = p.sem(f"at_xo_{u}")
    s_st = p.sem(f"at_st_{u}")
    s_xld = [p.sem(f"at_xld{j}_{u}") for j in range(2)]

    groups = [(0, 3), (3, 3), (6, 3), (9, 3), (12, 3), (15, 1)]

    def masked_kt(qt):
        return qt // 4, (qt % 4) * 128

    with p.nc.Block() as block:
        @block.gpsimd
        def _(g):
            wr = wp_dram.rearrange("(dc p) o -> p dc o", p=128)
            for dc in range(8):
                g.dma_start(out=v3(wp[:])[:, dc, :], in_=wr[:, dc, :]).then_inc(s_prw, 16)
            g.iota(dmat[:], [[-1, 128]], base=0, channel_multiplier=1,
                   allow_small_or_imprecise_dtypes=True)

        @block.sync
        def _(sync):
            sync.dma_start(out=bp_row[:], in_=bp_dram.rearrange("(a o) -> a o", a=1)
                           ).then_inc(s_prb, 16)
            sync.dma_start(out=rows[:], in_=qss_dram).then_inc(s_pre, 16)
            sync.dma_start(out=srow[0:1, 0:4], in_=kss4_dram).then_inc(s_pre, 16)
            sync.dma_start(out=srow[0:1, 7:8], in_=gamma_dram).then_inc(s_pre, 16)

    with p.nc.Block() as block:
        @block.vector
        def _(v):
            v.wait_ge(s_pre, 48)
            v.tensor_reduce(srow[0:1, 4:5], srow[0:1, 0:4], AXX, ALU.max).then_inc(s_zr, 1)

        @block.scalar
        def _(s):
            s.wait_ge(s_zr, 1)
            s.activation(srow[0:1, 5:6], srow[0:1, 4:5], AF.Sqrt, bias=0.0, scale=1.0)
            s.activation(rows[:], rows[:], AF.Sqrt, bias=0.0, scale=1.0)
            s.drain()
            s.activation(srow[0:1, 6:7], srow[0:1, 7:8], AF.Ln, bias=0.0, scale=1.0
                         ).then_inc(s_zr, 1)

        @block.vector
        def _(v):
            v.wait_ge(s_zr, 2)
            v.tensor_scalar(rows[:], rows[:], srow[0:1, 5:6], -0.03125, ALU.mult, ALU.mult)

    with p.nc.Block() as block:
        @block.tensor
        def _(t):
            t.matmul(psS[:, 0:1], c["ones"][0:1, :], srow[0:1, 6:7],
                     start=True, stop=True).then_inc(s_pmm, 1)

        @block.vector
        def _(v):
            v.wait_ge(s_pmm, 1)
            v.tensor_copy(zacc[:, 51:52], psS[:, 0:1]).then_inc(s_zr, 1)

        @block.scalar
        def _(s):
            s.wait_ge(s_zr, 3)
            s.activation(decay[:], dmat[:], AF.Exp, bias=0.0, scale=zacc[:, 51:52]
                         ).then_inc(s_zr, 1)

        @block.gpsimd
        def _(g):
            g.wait_ge(s_zr, 4)
            g.affine_select(decay[:], decay[:], [[-1, 128]], ALU.is_ge, 0.0,
                            base=0, channel_multiplier=1)

    emit_row_to_cols(p, c, rows, nmq, 16, psS)

    cE_track = []
    acc = 0
    for (q0, qn_) in groups:
        cE_track.append(acc)
        acc += 16 * qn_
    msk_track = {}
    mc = 0
    for gi, (q0, qn_) in enumerate(groups):
        for kt in range(16):
            for ql in range(qn_):
                if kt == masked_kt(q0 + ql)[0]:
                    mc += 1
                    msk_track[(gi, kt, ql)] = mc
    cS = [0]; cT = [0]; cK = [0]; cOE = [0]
    with p.nc.Block() as block:
        @block.sync
        def _(sync):
            for gi, (q0, qn_) in enumerate(groups):
                for kt in range(16):
                    kti = gi * 16 + kt
                    slot = kti % 2
                    if kti >= 2:
                        sync.wait_ge(s_ktd, kti - 1)
                    kr = kT_dram.rearrange("(dc p) t -> p dc t", p=128)
                    vr = v_dram.rearrange("(kk p) d -> p kk d", p=128)
                    sync.dma_start(out=v3(kt_b[slot][:])[:, :, :],
                                   in_=kr[:, :, kt * 512:(kt + 1) * 512]).then_inc(s_kv[slot], 16)
                    sync.dma_start(out=v3(vt_b[slot][:], dc=4)[:, :, :],
                                   in_=vr[:, kt * 4:(kt + 1) * 4, :]).then_inc(s_kv[slot], 16)

        @block.tensor
        def _(t):
            for gi, (q0, qn_) in enumerate(groups):
                if gi > 0:
                    t.wait_ge(s_oev, cOE[0])
                for kt in range(16):
                    kti = gi * 16 + kt
                    slot = kti % 2
                    t.wait_ge(s_kv[slot], 32 * (kti // 2 + 1))
                    for ql in range(qn_):
                        qt = q0 + ql
                        if cS[0] > 0:
                            t.wait_ge(s_exp, cS[0])
                        for dc in range(8):
                            last = t.matmul(psS[:], v3(qT_sb[:])[:, dc, qt * 128:(qt + 1) * 128],
                                            v3(kt_b[slot][:])[:, dc, :],
                                            start=(dc == 0), stop=(dc == 7))
                        last.then_inc(s_sg, 1)
                        cS[0] += 1
                        t.wait_ge(s_exp, cS[0])
                        for ks in range(4):
                            last = t.transpose(psT[:, ks * 128:(ks + 1) * 128],
                                               Pt[:, ks * 128:(ks + 1) * 128], c["ident_bf"][:])
                        last.then_inc(s_tp, 1)
                        cT[0] += 1
                        t.wait_ge(s_ptc, cT[0])
                        for ks in range(4):
                            for dvh in range(2):
                                last = t.matmul(psO[ql * 2 + dvh][:],
                                                PtT[:, ks * 128:(ks + 1) * 128],
                                                v3(vt_b[slot][:], dc=4)[:, ks, dvh * 512:(dvh + 1) * 512],
                                                start=(kt == 0 and ks == 0),
                                                stop=(kt == 15 and ks == 3))
                    last.then_inc(s_ktd, 1)
                    cK[0] += 1
                cOE[0] += 2 * qn_

        @block.vector
        def _(v):
            czr = 4
            for gi, (q0, qn_) in enumerate(groups):
                myE = cE_track[gi]
                for kt in range(16):
                    for ql in range(qn_):
                        qt = q0 + ql
                        mk, mco = masked_kt(qt)
                        sidx = myE + kt * qn_ + ql + 1
                        if kt == mk:
                            v.wait_ge(s_sg, sidx)
                            v.tensor_tensor(psS[:, mco:mco + 128], psS[:, mco:mco + 128],
                                            decay[:], ALU.mult).then_inc(s_msk, 1)
                        tidx = myE + kt * qn_ + ql + 1
                        v.wait_ge(s_tp, tidx)
                        v.tensor_copy(PtT[:], psT[:, 0:512]).then_inc(s_ptc, 1)
                for ql in range(qn_):
                    v.wait_ge(s_exp, myE + 16 * qn_)
                    v.tensor_reduce(zacc[:, 48 + ql:49 + ql], zacc[:, ql * 16:(ql + 1) * 16],
                                    AXX, ALU.add)
                    v.drain()
                    v.reciprocal(zacc[:, 48 + ql:49 + ql], zacc[:, 48 + ql:49 + ql]
                                 ).then_inc(s_zr, 1)
                    czr += 1

        @block.scalar
        def _(s):
            czr = 4
            coev = 0
            for gi, (q0, qn_) in enumerate(groups):
                myE = cE_track[gi]
                for kt in range(16):
                    for ql in range(qn_):
                        qt = q0 + ql
                        mk, mco = masked_kt(qt)
                        sidx = myE + kt * qn_ + ql + 1
                        if kt == mk:
                            s.wait_ge(s_msk, msk_track[(gi, kt, ql)])
                        else:
                            s.wait_ge(s_sg, sidx)
                        s.activation(Pt[:], psS[:], AF.Exp, bias=nmq[:, qt:qt + 1],
                                     scale=p.cc(0.03125),
                                     accum_out=zacc[:, ql * 16 + kt:ql * 16 + kt + 1]
                                     ).then_inc(s_exp, 1)
                s.wait_ge(s_ktd, (gi + 1) * 16)
                for ql in range(qn_):
                    s.wait_ge(s_zr, czr + ql + 1)
                    for dvh in range(2):
                        s.activation(v3(O_sb[:], dc=16)[:, q0 + ql, dvh * 512:(dvh + 1) * 512],
                                     psO[ql * 2 + dvh][:], AF.Identity, bias=0.0,
                                     scale=zacc[:, 48 + ql:49 + ql]).then_inc(s_oev, 1)
                        coev += 1
                czr += qn_

    with p.nc.Block() as block:
        @block.tensor
        def _(t):
            for tt in range(NT):
                if tt >= 2:
                    t.wait_ge(s_occ, tt - 1)
                for dc in range(8):
                    last = t.transpose(psT[:, dc * 128:(dc + 1) * 128],
                                       v3(O_sb[:], dc=16)[:, tt, dc * 128:(dc + 1) * 128],
                                       c["ident_bf"][:])
                last.then_inc(s_otr, 1)
                t.wait_ge(s_occ, tt + 1)
                if tt >= 2:
                    t.wait_ge(s_xo, tt - 1)
                t.wait_ge(s_prw, 16 * 8)
                t.wait_ge(s_prb, 16)
                for dvh in range(2):
                    for dc in range(8):
                        t.matmul(psO[2 * (tt % 2) + dvh][:],
                                 v3(OT[:])[:, dc, tt * 128:(tt + 1) * 128],
                                 v3(wp[:])[:, dc, dvh * 512:(dvh + 1) * 512],
                                 start=(dc == 0), stop=False)
                    last = t.matmul(psO[2 * (tt % 2) + dvh][:], c["ones"][0:1, 0:128],
                                    bp_row[0:1, dvh * 512:(dvh + 1) * 512],
                                    start=False, stop=True)
                last.then_inc(s_pmm, 1)

        @block.vector
        def _(v):
            def xadd(tt):
                b = (tt % 2) * 1024
                v.wait_ge(s_pmm, 1 + tt + 1)
                v.wait_ge(s_xld[tt % 2], 16 * (tt // 2 + 1))
                if tt >= 2:
                    v.wait_ge(s_st, 16 * (tt - 1))
                for dvh in range(2):
                    last = v.tensor_tensor(xo[:, b + dvh * 512:b + (dvh + 1) * 512],
                                           psO[2 * (tt % 2) + dvh][:],
                                           xa[:, b + dvh * 512:b + (dvh + 1) * 512], ALU.add)
                last.then_inc(s_xo, 1)
            for tt in range(NT):
                v.wait_ge(s_otr, tt + 1)
                v.tensor_copy(v3(OT[:])[:, :, tt * 128:(tt + 1) * 128],
                              v3(psT[:])).then_inc(s_occ, 1)
                if tt >= 1:
                    xadd(tt - 1)
            xadd(NT - 1)

        @block.sync
        def _(sync):
            def store(tt):
                b = (tt % 2) * 1024
                sync.wait_ge(s_xo, tt + 1)
                sync.dma_start(out=xout_dram[tt * 128:(tt + 1) * 128, :],
                               in_=xo[:, b:b + 1024]).then_inc(s_st, 16)
            for tt in range(NT):
                b = (tt % 2) * 1024
                if tt >= 2:
                    sync.wait_ge(s_xo, tt - 1)
                sync.dma_start(out=xa[:, b:b + 1024],
                               in_=x_dram[tt * 128:(tt + 1) * 128, :]).then_inc(s_xld[tt % 2], 16)
                if tt >= 1:
                    store(tt - 1)
            store(NT - 1)
            sync.wait_ge(s_st, 16 * NT)
    return


def emit_ffn0(p, c, xnT, win_dram, bin_dram, w1_dram, b1_dram, w2_dram, b2_dram,
              wout_dram, bout_dram, x_dram, xout_dram):
    u = p.uid()
    win_sb = p.sb(f"f0_win_{u}", [128, 8 * 64], BF16)
    wout_sb = p.sb(f"f0_wout_{u}", [128, 1024], BF16)
    cols16 = p.sb(f"f0_c16_{u}", [128, 50], F32)
    bout_row = p.sb(f"f0_bo_{u}", [1, 1024], F32)
    s_sb = p.sb(f"f0_s_{u}", [128, 1024], F32)
    h_sb = p.sb(f"f0_h_{u}", [128, 2 * 1024], F32)
    g_sb = p.sb(f"f0_g_{u}", [128, 1024], F32)
    g_bf = p.sb(f"f0_gbf_{u}", [128, 1024], BF16)
    xa = p.sb(f"f0_xa_{u}", [128, 2 * 1024], F32)
    xo = p.sb(f"f0_xo_{u}", [128, 2 * 1024], F32)
    ps = [p.ps(f"f0_ps{j}_{u}", [128, 512], F32) for j in range(4)]

    s_wld = p.sem(f"f0_wld_{u}")
    s_bld = p.sem(f"f0_bld_{u}")
    s_smm = p.sem(f"f0_smm_{u}")
    s_sev = p.sem(f"f0_sev_{u}")
    s_h = p.sem(f"f0_h_{u}")
    s_g = p.sem(f"f0_g_{u}")
    s_pmm = p.sem(f"f0_pmm_{u}")
    s_xld = [p.sem(f"f0_xld{j}_{u}") for j in range(2)]
    s_xo = p.sem(f"f0_xo_{u}")
    s_st = p.sem(f"f0_st_{u}")

    with p.nc.Block() as block:
        @block.gpsimd
        def _(g):
            wr = win_dram.rearrange("(dc p) o -> p dc o", p=128)
            for dc in range(8):
                g.dma_start(out=v3(win_sb[:])[:, dc, :], in_=wr[:, dc, :]).then_inc(s_wld, 16)
            for hh in range(2):
                g.dma_start(out=wout_sb[hh * 64:(hh + 1) * 64, :], in_=wout_dram
                            ).then_inc(s_wld, 16)

        @block.sync
        def _(sync):
            for hh in range(2):
                pb = hh * 64
                sync.dma_start(out=cols16[pb:pb + 64, 0:16], in_=w1_dram).then_inc(s_bld, 16)
                sync.dma_start(out=cols16[pb:pb + 64, 16:32], in_=b1_dram).then_inc(s_bld, 16)
                sync.dma_start(out=cols16[pb:pb + 64, 32:48], in_=w2_dram).then_inc(s_bld, 16)
                sync.dma_start(out=cols16[pb:pb + 64, 48:49],
                               in_=b2_dram.rearrange("(p a) -> p a", a=1)).then_inc(s_bld, 16)
                sync.dma_start(out=cols16[pb:pb + 64, 49:50],
                               in_=bin_dram.rearrange("(p a) -> p a", a=1)).then_inc(s_bld, 16)
            sync.dma_start(out=bout_row[:], in_=bout_dram.rearrange("(a o) -> a o", a=1)
                           ).then_inc(s_bld, 16)

    with p.nc.Block() as block:
        @block.tensor
        def _(t):
            t.wait_ge(s_wld, 16 * 10)
            t.wait_ge(s_bld, 16 * 11)
            for grp in range(2):
                for hh in range(2):
                    tr = hh * 1024 + grp * 512
                    for dc in range(8):
                        last = t.matmul(ps[grp][hh * 64:(hh + 1) * 64, :],
                                        v3(win_sb[:])[:, dc, 0:64],
                                        v3(xnT[:])[:, dc, tr:tr + 512],
                                        start=(dc == 0), stop=(dc == 7))
                last.then_inc(s_smm, 1)

        @block.scalar
        def _(sc):
            for grp in range(2):
                sc.wait_ge(s_smm, grp + 1)
                sc.activation(s_sb[:, grp * 512:(grp + 1) * 512], ps[grp][:],
                              AF.Identity, bias=cols16[:, 49:50], scale=1.0).then_inc(s_sev, 1)

    with p.nc.Block() as block:
        @block.scalar
        def _(sc):
            for i in range(16):
                if i >= 2:
                    sc.wait_ge(s_g, i - 1)
                sc.activation(h_sb[:, (i % 2) * 1024:(i % 2 + 1) * 1024], s_sb[:],
                              AF.Relu, bias=cols16[:, 16 + i:17 + i],
                              scale=cols16[:, i:i + 1]).then_inc(s_h, 1)

        @block.vector
        def _(v):
            for i in range(16):
                v.wait_ge(s_h, i + 1)
                hb = h_sb[:, (i % 2) * 1024:(i % 2 + 1) * 1024]
                if i == 0:
                    v.tensor_scalar(g_sb[:], hb, cols16[:, 32:33], None, ALU.mult
                                    ).then_inc(s_g, 1)
                else:
                    v.scalar_tensor_tensor(g_sb[:], hb, cols16[:, 32 + i:33 + i], g_sb[:],
                                           ALU.mult, ALU.add).then_inc(s_g, 1)
                v.drain()
            v.wait_ge(s_g, 16)
            v.drain()
            v.tensor_scalar(g_bf[:], g_sb[:], cols16[:, 48:49], None, ALU.add)

    with p.nc.Block() as block:
        @block.tensor
        def _(t):
            for tt in range(NT):
                hh = tt // 8
                gcol = (tt % 8) * 128
                if tt >= 2:
                    t.wait_ge(s_xo, tt - 1)
                for dvh in range(2):
                    t.matmul(ps[2 * (tt % 2) + dvh][:],
                             g_bf[hh * 64:(hh + 1) * 64, gcol:gcol + 128],
                             wout_sb[hh * 64:(hh + 1) * 64, dvh * 512:(dvh + 1) * 512],
                             start=True, stop=False)
                    last = t.matmul(ps[2 * (tt % 2) + dvh][:], c["ones"][0:1, 0:128],
                                    bout_row[0:1, dvh * 512:(dvh + 1) * 512],
                                    start=False, stop=True)
                last.then_inc(s_pmm, 1)

        @block.vector
        def _(v):
            for tt in range(NT):
                b = (tt % 2) * 1024
                v.wait_ge(s_pmm, tt + 1)
                v.wait_ge(s_xld[tt % 2], 16 * (tt // 2 + 1))
                if tt >= 2:
                    v.wait_ge(s_st, 16 * (tt - 1))
                for dvh in range(2):
                    last = v.tensor_tensor(xo[:, b + dvh * 512:b + (dvh + 1) * 512],
                                           ps[2 * (tt % 2) + dvh][:],
                                           xa[:, b + dvh * 512:b + (dvh + 1) * 512], ALU.add)
                last.then_inc(s_xo, 1)

        @block.sync
        def _(sync):
            def store(tt):
                b = (tt % 2) * 1024
                sync.wait_ge(s_xo, tt + 1)
                sync.dma_start(out=xout_dram[tt * 128:(tt + 1) * 128, :],
                               in_=xo[:, b:b + 1024]).then_inc(s_st, 16)
            for tt in range(NT):
                b = (tt % 2) * 1024
                if tt >= 2:
                    sync.wait_ge(s_xo, tt - 1)
                sync.dma_start(out=xa[:, b:b + 1024],
                               in_=x_dram[tt * 128:(tt + 1) * 128, :]).then_inc(s_xld[tt % 2], 16)
                if tt >= 1:
                    store(tt - 1)
            store(NT - 1)
            sync.wait_ge(s_st, 16 * NT)
    return


class Seq:

    def __init__(self, p, block, name):
        self.p = p
        self.block = block
        self.s = {}
        self.n = {}
        self.name = name

    def _sem(self, e):
        if e not in self.s:
            self.s[e] = self.p.sem(f"sq_{self.name}_{e}")
            self.n[e] = 0
        return self.s[e]

    def on(self, engname, *deps):
        self._sem(engname)

        def runner(fn):
            def body(eng):
                need_drain = False
                for (en, cnt) in deps:
                    if en == engname:
                        need_drain = True
                    else:
                        eng.wait_ge(self.s[en], cnt)
                if need_drain:
                    eng.drain()
                last = fn(eng)
                assert last is not None, f"emit_fn must return last inst ({engname})"
                last.then_inc(self.s[engname], 1)
            getattr(self.block, engname)(body)
            self.n[engname] += 1
            return (engname, self.n[engname])
        return runner

    def tok(self, engname):
        return (engname, self.n[engname])


def emit_gla_pass1(p, c, xnT, wq_dram, wf_dram, wi_dram,
                   oT_dram, qbT_dram, send_dram, gtot_dram, gch_dram,
                   mA_dram, mA2_dram, mB_dram, mC_dram,
                   sel8_dram, sel8p_dram, e8t_dram):
    u = p.uid()
    NCH = 16
    wsb = {nm: p.sb(f"g1_{nm}_{u}", [128, 8 * 1024], BF16) for nm in ("wq", "wf", "wi")}
    Eq = p.sb(f"g1_Eq_{u}", [128, 1024], F32)
    Ef = p.sb(f"g1_Ef_{u}", [128, 1024], F32)
    flt = p.sb(f"g1_fl_{u}", [128, 1024], F32)
    Fc = p.sb(f"g1_F_{u}", [128, 1024], F32)
    kk = p.sb(f"g1_k_{u}", [128, 1024], F32)
    kkb = p.sb(f"g1_kkb_{u}", [128, 1024], BF16)
    vt = p.sb(f"g1_vt_{u}", [128, 1024], BF16)
    qbb = p.sb(f"g1_qbb_{u}", [128, 1024], BF16)
    ktC = p.sb(f"g1_ktC_{u}", [128, 1024], BF16)
    qbT = p.sb(f"g1_qbT_{u}", [128, 1024], BF16)
    EqT = p.sb(f"g1_EqT_{u}", [128, 1024], BF16)
    kT_ = p.sb(f"g1_kT_{u}", [128, 1024], BF16)
    FTx = p.sb(f"g1_FTx_{u}", [128, 8 * 132], F32)
    argb = p.sb(f"g1_arg_{u}", [128, 1024], F32)
    expb = p.sb(f"g1_exp_{u}", [128, 1024], BF16)
    vA_q = p.sb(f"g1_vAq_{u}", [128, 1024], BF16)
    vA_k = p.sb(f"g1_vAk_{u}", [128, 1024], BF16)
    vA2k = p.sb(f"g1_vA2k_{u}", [128, 1024], BF16)
    vB_q = p.sb(f"g1_vBq_{u}", [128, 1024], BF16)
    vB_k = p.sb(f"g1_vBk_{u}", [128, 1024], BF16)
    Pm = p.sb(f"g1_Pm_{u}", [128, 2048], BF16)
    Pstg = p.sb(f"g1_Ps_{u}", [128, 2048], BF16)
    DSTz = p.sb(f"g1_DSz_{u}", [128, 128], BF16)
    DSTzf = p.sb(f"g1_DSzf_{u}", [128, 128], F32)
    DSTc = p.sb(f"g1_DSc_{u}", [128, 128], BF16)
    DSTm = p.sb(f"g1_DSm_{u}", [128, 128], BF16)
    dsc = p.sb(f"g1_dsc_{u}", [128, 128], F32)
    masks = {nm: p.sb(f"g1_{nm}_{u}", [128, 128], BF16)
             for nm in ("mA", "mA2", "mB", "mC", "mCn")}
    sel8 = p.sb(f"g1_s8_{u}", [128, 16], F32)
    sel8p = p.sb(f"g1_s8p_{u}", [128, 16], F32)
    e8t = p.sb(f"g1_e8t_{u}", [16, 128], F32)
    zerobf = p.sb(f"g1_zb_{u}", [128, 2048], BF16)
    Fb = p.sb(f"g1_Fb_{u}", [16, 1024], F32)
    Fbp = p.sb(f"g1_Fbp_{u}", [16, 1024], F32)
    ddrow = p.sb(f"g1_ddr_{u}", [16, 1024], F32)
    ddT = p.sb(f"g1_ddT_{u}", [128, 128], F32)
    brow = p.sb(f"g1_brow_{u}", [1, 1024], F32)
    S = p.sb(f"g1_S_{u}", [128, 1024], F32)
    Sbf = p.sb(f"g1_Sbf_{u}", [128, 1024], BF16)
    Gtot = p.sb(f"g1_Gto_{u}", [128, 8], F32)
    gch_sb = p.sb(f"g1_gch_{u}", [128, 8 * NCH], F32)
    ot_st = p.sb(f"g1_ot_{u}", [128, 1024], F32)
    cols = p.sb(f"g1_co_{u}", [128, 48], F32)
    psA = p.ps(f"g1_psA_{u}", [128, 1024], F32)
    psT = p.ps(f"g1_psT_{u}", [128, 1024], BF16)
    psP = p.ps(f"g1_psP_{u}", [128, 2048], F32)
    psM = p.ps(f"g1_psM_{u}", [128, 512], F32)
    s_wld = p.sem(f"g1_wld_{u}")
    s_mld = p.sem(f"g1_mld_{u}")
    s_st = p.sem(f"g1_st_{u}")
    n_st = [0]

    (ZQ, RZQ, ZF, RZF, LSE, ZETA, ZC, T1, T2, ZE8, ZB8, ZCB, QS, RF) = range(14)
    GCS, EGCS = 16, 24

    with p.nc.Block() as block:
        @block.gpsimd
        def _(g):
            for nm, wd in (("wq", wq_dram), ("wf", wf_dram), ("wi", wi_dram)):
                wr = wd.rearrange("(dc p) o -> p dc o", p=128)
                for dc in range(8):
                    g.dma_start(out=v3(wsb[nm][:])[:, dc, :], in_=wr[:, dc, :]
                                ).then_inc(s_wld, 16)
            for nm, d_ in (("mA", mA_dram), ("mA2", mA2_dram), ("mB", mB_dram),
                           ("mC", mC_dram)):
                g.dma_start(out=masks[nm][:], in_=d_).then_inc(s_wld, 16)

        @block.sync
        def _(sy):
            sy.dma_start(out=sel8[:], in_=sel8_dram).then_inc(s_mld, 16)
            sy.dma_start(out=sel8p[:], in_=sel8p_dram).then_inc(s_mld, 16)
            sy.dma_start(out=e8t[:], in_=e8t_dram).then_inc(s_mld, 16)

        @block.vector
        def _(v):
            v.memset(S[:], 0.0)
            v.memset(Sbf[:], 0.0)
            v.memset(Gtot[:], 0.0)
            v.memset(zerobf[:], 0.0)
            v.memset(FTx[:], 0.0)
            v.wait_ge(s_wld, 16 * 28)
            v.tensor_scalar(masks["mCn"][:], masks["mC"][:], -1.0, 1.0,
                            ALU.mult, ALU.add)

    orr = oT_dram.rearrange("(ci p) t -> p ci t", p=128)
    qrr = qbT_dram.rearrange("(ci p) t -> p ci t", p=128)
    FT3 = v3(FTx[:], dc=8)
    FTx_h = FTx

    def ref_ap(off, blk):
        return bass.AP(FTx_h, off,
                       [[8 * 132, 128], [132, 8], [blk, 128 // blk], [0, blk]])

    def cur_ap(blk):
        return bass.AP(FTx_h, 1,
                       [[8 * 132, 128], [132, 8], [blk, 128 // blk], [1, blk]])


    with p.nc.Block() as block:
        seq = Seq(p, block, f"g1_{u}")
        prev_state = [None]
        for ch in range(NCH):
            def mk_proj(nm, extra_wait=False):
                def fn(t, nm=nm, ch=ch, extra_wait=extra_wait):
                    if extra_wait:
                        t.wait_ge(s_wld, 16 * 28)
                        t.wait_ge(s_mld, 48)
                    for oh in range(2):
                        for dc in range(8):
                            last = t.matmul(psA[:, oh * 512:(oh + 1) * 512],
                                            v3(xnT[:])[:, dc, ch * 128:(ch + 1) * 128],
                                            v3(wsb[nm][:])[:, dc, oh * 512:(oh + 1) * 512],
                                            start=(dc == 0), stop=(dc == 7))
                    return last
                return fn

            deps = [prev_state[0]] if prev_state[0] else []
            tv = seq.on("tensor", *deps)(mk_proj("wi", extra_wait=(ch == 0)))
            ev = seq.on("vector", tv)(lambda v: v.tensor_copy(vt[:], psA[:]))
            tf = seq.on("tensor", ev)(mk_proj("wf"))
            evf = seq.on("vector", tf)(lambda v: v.tensor_copy(flt[:], psA[:]))

            def fexp(sc):
                sc.activation(Ef[:, 0:512], psA[:, 0:512], AF.Exp, bias=0.0, scale=1.0,
                              accum_out=cols[:, T1:T1 + 1])
                return sc.activation(Ef[:, 512:1024], psA[:, 512:1024], AF.Exp, bias=0.0,
                                     scale=1.0, accum_out=cols[:, T2:T2 + 1])
            exf = seq.on("scalar", tf)(fexp)
            zf = seq.on("vector", exf, evf)(lambda v: v.tensor_tensor(
                cols[:, ZF:ZF + 1], cols[:, T1:T1 + 1], cols[:, T2:T2 + 1], ALU.add))
            tq = seq.on("tensor", evf, exf)(mk_proj("wq"))

            def qexp(sc):
                sc.activation(Eq[:, 0:512], psA[:, 0:512], AF.Exp, bias=0.0, scale=1.0,
                              accum_out=cols[:, T1:T1 + 1])
                return sc.activation(Eq[:, 512:1024], psA[:, 512:1024], AF.Exp, bias=0.0,
                                     scale=1.0, accum_out=cols[:, T2:T2 + 1])
            exq = seq.on("scalar", tq, zf)(qexp)
            zq = seq.on("vector", exq)(lambda v: v.tensor_tensor(
                cols[:, ZQ:ZQ + 1], cols[:, T1:T1 + 1], cols[:, T2:T2 + 1], ALU.add))

            def cums(t):
                for oh in range(2):
                    t.matmul(psA[:, oh * 512:(oh + 1) * 512], c["LT"][:],
                             flt[:, oh * 512:(oh + 1) * 512], start=True, stop=True)
                for ci in range(8):
                    last = t.matmul(psM[:, 64 + ci:65 + ci], flt[:, ci * 128:(ci + 1) * 128],
                                    c["ones"][:, 0:1], start=True, stop=True)
                return last
            tF = seq.on("tensor", exq)(cums)
            evF = seq.on("vector", tF, zq)(lambda v: (
                v.tensor_copy(Fc[:], psA[:]),
                v.tensor_copy(cols[:, GCS:GCS + 8], psM[:, 64:72]))[-1])
            lse = seq.on("scalar", zq)(lambda sc: sc.activation(
                cols[:, LSE:LSE + 1], cols[:, ZF:ZF + 1], AF.Ln, bias=0.0, scale=1.0))
            tz = seq.on("tensor", lse, evF)(lambda t: t.matmul(
                psM[:, 0:1], c["LT"][:], cols[:, LSE:LSE + 1], start=True, stop=True))
            evz = seq.on("vector", tz)(lambda v: v.tensor_copy(
                cols[:, ZETA:ZETA + 1], psM[:, 0:1]))

            def extr(t):
                t.matmul(psM[0:16, 4:5], sel8[:], cols[:, ZETA:ZETA + 1],
                         start=True, stop=True)
                t.matmul(psM[0:16, 8:9], sel8p[:], cols[:, ZETA:ZETA + 1],
                         start=True, stop=True)
                t.matmul(psM[0:1, 12:13], c["e127"][:], cols[:, ZETA:ZETA + 1],
                         start=True, stop=True)
                for oh in range(2):
                    t.matmul(psP[0:16, oh * 512:(oh + 1) * 512], sel8[:],
                             Fc[:, oh * 512:(oh + 1) * 512], start=True, stop=True)
                for oh in range(2):
                    last = t.matmul(psP[0:16, 1024 + oh * 512:1024 + (oh + 1) * 512],
                                    sel8p[:], Fc[:, oh * 512:(oh + 1) * 512],
                                    start=True, stop=True)
                return last
            tx1 = seq.on("tensor", evz)(extr)

            def evx(v):
                v.tensor_copy(cols[0:16, T1:T1 + 1], psM[0:16, 4:5])
                v.tensor_copy(cols[0:16, T2:T2 + 1], psM[0:16, 8:9])
                v.tensor_copy(cols[0:1, ZC:ZC + 1], psM[0:1, 12:13])
                v.tensor_copy(Fb[:], psP[0:16, 0:1024])
                return v.tensor_copy(Fbp[:], psP[0:16, 1024:2048])
            ex1 = seq.on("vector", tx1)(evx)

            def extr2(t):
                for oh in range(2):
                    last = t.matmul(psP[0:1, oh * 512:(oh + 1) * 512], c["e127"][:],
                                    Fc[:, oh * 512:(oh + 1) * 512], start=True, stop=True)
                return last
            tx1b = seq.on("tensor", ex1)(extr2)
            ex1b = seq.on("vector", tx1b)(lambda v: v.tensor_copy(
                brow[:], psP[0:1, 0:1024]))

            def tbc(t):
                t.matmul(psM[:, 16:17], e8t[0:16, :], cols[0:16, T1:T1 + 1],
                         start=True, stop=True)
                t.matmul(psM[:, 20:21], e8t[0:16, :], cols[0:16, T2:T2 + 1],
                         start=True, stop=True)
                return t.matmul(psM[:, 24:25], c["ones"][0:1, :], cols[0:1, ZC:ZC + 1],
                                start=True, stop=True)
            tx2 = seq.on("tensor", ex1b)(tbc)

            def gcc(v, ch=ch):
                v.tensor_copy(cols[:, ZE8:ZE8 + 1], psM[:, 16:17])
                v.tensor_copy(cols[:, ZB8:ZB8 + 1], psM[:, 20:21])
                v.tensor_copy(cols[:, ZCB:ZCB + 1], psM[:, 24:25])
                v.drain()
                v.tensor_scalar(cols[:, GCS:GCS + 8], cols[:, GCS:GCS + 8],
                                cols[:, ZCB:ZCB + 1], None, ALU.subtract)
                v.drain()
                v.tensor_copy(gch_sb[:, ch * 8:(ch + 1) * 8], cols[:, GCS:GCS + 8])
                v.tensor_tensor(Gtot[:], Gtot[:], cols[:, GCS:GCS + 8], ALU.add)
                v.tensor_tensor(ddrow[:], Fb[:], Fbp[:], ALU.subtract)
                v.tensor_tensor(cols[0:16, QS:QS + 1], cols[0:16, T1:T1 + 1],
                                cols[0:16, T2:T2 + 1], ALU.subtract)
                v.drain()
                return v.tensor_scalar(ddrow[:], ddrow[:], cols[0:16, QS:QS + 1],
                                       None, ALU.subtract)
            gc1 = seq.on("vector", tx2)(gcc)

            def gcexp(sc):
                sc.activation(cols[:, EGCS:EGCS + 8], cols[:, GCS:GCS + 8], AF.Exp,
                              bias=0.0, scale=1.0)
                return sc.activation(ddrow[:], ddrow[:], AF.Exp, bias=0.0, scale=1.0)
            gc2 = seq.on("scalar", gc1)(gcexp)

            def tddf(t):
                for cc in range(8):
                    last = t.transpose(psP[:, cc * 16:(cc + 1) * 16],
                                       ddrow[0:16, cc * 128:(cc + 1) * 128],
                                       c["ident"][0:16, 0:16])
                return last
            td1 = seq.on("tensor", gc2)(tddf)
            ed1 = seq.on("vector", td1)(lambda v: v.tensor_copy(ddT[:], psP[:, 0:128]))

            def tFT(t):
                for cc in range(8):
                    last = t.transpose(psP[:, 1024 + cc * 128:1024 + (cc + 1) * 128],
                                       Fc[:, cc * 128:(cc + 1) * 128], c["ident"][:])
                return last
            tf1 = seq.on("tensor", ed1)(tFT)
            ef1 = seq.on("vector", tf1)(lambda v: v.tensor_copy(
                FT3[:, :, 1:129], psP[:, 1024:2048].rearrange("p (cc t) -> p cc t", cc=8)))

            def hats(v):
                v.reciprocal(cols[:, RZQ:RZQ + 1], cols[:, ZQ:ZQ + 1])
                v.reciprocal(cols[:, RZF:RZF + 1], cols[:, ZF:ZF + 1])
                v.drain()
                v.tensor_scalar(Ef[:], Ef[:], cols[:, RZF:RZF + 1], None, ALU.mult)
                v.drain()
                v.tensor_scalar(kk[:], Ef[:], -1.0, 1.0, ALU.mult, ALU.add)
                v.drain()
                v.tensor_copy(kkb[:], kk[:])
                v.tensor_scalar(EqT[:], Eq[:], cols[:, RZQ:RZQ + 1], 0.125,
                                ALU.mult, ALU.mult)
                return v.tensor_copy(kT_[:], kk[:])
            h1 = seq.on("vector", gc2)(hats)

            def tq1(t):
                for cc in range(8):
                    last = t.transpose(psT[:, cc * 128:(cc + 1) * 128],
                                       EqT[:, cc * 128:(cc + 1) * 128], c["ident_bf"][:])
                return last
            t1_ = seq.on("tensor", h1, ef1)(tq1)
            e1_ = seq.on("vector", t1_)(lambda v: v.tensor_copy(EqT[:], psT[:]))

            def tk1(t):
                for cc in range(8):
                    last = t.transpose(psT[:, cc * 128:(cc + 1) * 128],
                                       kT_[:, cc * 128:(cc + 1) * 128], c["ident_bf"][:])
                return last
            t2_ = seq.on("tensor", e1_)(tk1)
            e2_ = seq.on("vector", t2_)(lambda v: v.tensor_copy(kT_[:], psT[:]))

            last_tok = e2_
            for (dst, off, blk, qside) in ((vA_q, 0, 4, True), (vA_k, 0, 4, False),
                                           (vA2k, 4, 4, False), (vB_q, 0, 8, True),
                                           (vB_k, 8, 8, False)):
                def mk_sub(off=off, blk=blk, qside=qside):
                    def fn(v):
                        a4 = argb[:].rearrange("p (cc a b) -> p cc a b", cc=8, b=blk)
                        if qside:
                            return v.tensor_tensor(a4, cur_ap(blk), ref_ap(off, blk),
                                                   ALU.subtract)
                        return v.tensor_tensor(a4, ref_ap(off, blk), cur_ap(blk),
                                               ALU.subtract)
                    return fn
                st1 = seq.on("vector", last_tok)(mk_sub())
                st2 = seq.on("scalar", st1)(lambda sc: sc.activation(
                    expb[:], argb[:], AF.Exp, bias=0.0, scale=1.0))

                def mk_mul(dst=dst, qside=qside):
                    def fn(v):
                        src = EqT if qside else kT_
                        return v.tensor_tensor(dst[:], expb[:], src[:], ALU.mult)
                    return fn
                last_tok = seq.on("vector", st2)(mk_mul())

            dz1 = seq.on("vector", gc2)(lambda v: v.tensor_scalar(
                dsc[:], c["ident"][:], cols[:, ZETA:ZETA + 1], None, ALU.mult))
            tdz = seq.on("tensor", dz1, last_tok)(lambda t: t.matmul(
                psM[:, 128:256], c["ones"][:], dsc[:], start=True, stop=True))
            dz2 = seq.on("vector", tdz)(lambda v: v.tensor_scalar(
                DSTzf[:], psM[:, 128:256], cols[:, ZETA:ZETA + 1], None, ALU.subtract))
            dz3 = seq.on("gpsimd", dz2)(lambda g: g.affine_select(
                DSTzf[:], DSTzf[:], [[1, 128]], ALU.is_ge, 1e30, base=0,
                channel_multiplier=-1))
            dz4 = seq.on("scalar", dz3)(lambda sc: sc.activation(
                DSTz[:], DSTzf[:], AF.Exp, bias=0.0, scale=-1.0))

            def dcp(v):
                v.tensor_tensor(cols[:, QS:QS + 1], cols[:, ZB8:ZB8 + 1],
                                cols[:, ZETA:ZETA + 1], ALU.subtract)
                v.tensor_tensor(cols[:, RF:RF + 1], cols[:, ZETA:ZETA + 1],
                                cols[:, ZE8:ZE8 + 1], ALU.subtract)
                v.drain()
                return v.tensor_scalar(dsc[:], c["ident"][:], cols[:, QS:QS + 1],
                                       None, ALU.mult)
            dc1 = seq.on("vector", dz4)(dcp)
            tdc = seq.on("tensor", dc1)(lambda t: t.matmul(
                psM[:, 128:256], c["ones"][:], dsc[:], start=True, stop=True))
            dc2 = seq.on("vector", tdc)(lambda v: v.tensor_scalar(
                DSTzf[:], psM[:, 128:256], cols[:, RF:RF + 1], None, ALU.add))
            dc3 = seq.on("scalar", dc2)(lambda sc: sc.activation(
                DSTc[:], DSTzf[:], AF.Exp, bias=0.0, scale=1.0))

            def dmrg(v):
                v.tensor_tensor(DSTc[:], DSTc[:], masks["mC"][:], ALU.mult)
                v.tensor_tensor(DSTm[:], DSTz[:], masks["mCn"][:], ALU.mult)
                v.drain()
                return v.tensor_tensor(DSTm[:], DSTm[:], DSTc[:], ALU.add)
            dmg = seq.on("vector", dc3)(dmrg)

            def mk_pass(kvar, qvar):
                def fn(t, kvar=kvar, qvar=qvar):
                    for h in range(0, 16, 2):
                        t.matmul(psP[:, h * 128:(h + 1) * 128],
                                 v3(kvar[:])[0:64, h // 2, :],
                                 v3(qvar[:])[0:64, h // 2, :], start=True, stop=True)
                    t.matmul(psM[0:1, 448:449], c["ones_bf"][:, 0:1],
                             c["ones_bf"][:, 0:1], start=True, stop=True)
                    for h in range(1, 16, 2):
                        last = t.matmul(psP[:, h * 128:(h + 1) * 128],
                                        v3(kvar[:])[64:128, h // 2, :],
                                        v3(qvar[:])[64:128, h // 2, :],
                                        start=True, stop=True)
                    return last
                return fn

            def mk_merge(mname, first):
                def fn(v, mname=mname, first=first):
                    v.tensor_copy(Pstg[:], psP[:])
                    v.drain()
                    p3 = Pstg[:].rearrange("p (h t) -> p h t", h=16)
                    m3 = masks[mname][:].rearrange("p (o t) -> p o t", o=1)
                    a1, a2 = bass.broadcast_tensor_aps(p3, m3)
                    if first:
                        o3 = Pm[:].rearrange("p (h t) -> p h t", h=16)
                        return v.tensor_tensor(o3, a1, a2, ALU.mult)
                    v.tensor_tensor(a1, a1, a2, ALU.mult)
                    v.drain()
                    return v.tensor_tensor(Pm[:], Pm[:], Pstg[:], ALU.add)
                return fn

            tok = seq.on("tensor", dmg, last_tok)(mk_pass(vA_k, vA_q))
            tok = seq.on("vector", tok)(mk_merge("mA", True))
            tok = seq.on("tensor", tok)(mk_pass(vA2k, vA_q))
            tok = seq.on("vector", tok)(mk_merge("mA2", False))
            tok = seq.on("tensor", tok)(mk_pass(vB_k, vB_q))
            tok = seq.on("vector", tok)(mk_merge("mB", False))

            for I in range(2, 16):
                def adv(v, I=I):
                    k3 = vB_k[:].rearrange("p (cc s) -> p cc s", cc=8)[:, :, 0:8 * (I - 1)]
                    d3 = ddT[:].rearrange("p (cc j) -> p cc j", cc=8)[:, :, I - 1:I]
                    a1, a2 = bass.broadcast_tensor_aps(k3, d3)
                    return v.tensor_tensor(a1, a1, a2, ALU.mult)
                tok = seq.on("vector", tok)(adv)

                def cmms(t, I=I):
                    ns = 8 * (I - 1)
                    for h in range(0, 16, 2):
                        t.matmul(psP[0:ns, h * 128 + 8 * I:h * 128 + 8 * I + 8],
                                 v3(vB_k[:])[0:64, h // 2, 0:ns],
                                 v3(vB_q[:])[0:64, h // 2, 8 * I:8 * I + 8],
                                 start=True, stop=True)
                    t.matmul(psM[0:1, 448:449], c["ones_bf"][:, 0:1],
                             c["ones_bf"][:, 0:1], start=True, stop=True)
                    for h in range(1, 16, 2):
                        last = t.matmul(psP[0:ns, h * 128 + 8 * I:h * 128 + 8 * I + 8],
                                        v3(vB_k[:])[64:128, h // 2, 0:ns],
                                        v3(vB_q[:])[64:128, h // 2, 8 * I:8 * I + 8],
                                        start=True, stop=True)
                    return last
                tok = seq.on("tensor", tok)(cmms)
            tok = seq.on("vector", tok)(mk_merge("mC", False))

            def dapp(v):
                o3 = Pm[:].rearrange("p (h t) -> p h t", h=16)
                m3 = DSTm[:].rearrange("p (o t) -> p o t", o=1)
                a1, a2 = bass.broadcast_tensor_aps(o3, m3)
                return v.tensor_tensor(a1, a1, a2, ALU.mult)
            pdone = seq.on("vector", tok)(dapp)

            qb0 = seq.on("vector", pdone)(lambda v: v.tensor_scalar(
                argb[:], Fc[:], cols[:, ZETA:ZETA + 1], None, ALU.subtract))
            qb1 = seq.on("scalar", qb0)(lambda sc: sc.activation(
                argb[:], argb[:], AF.Exp, bias=0.0, scale=1.0))

            def qb2f(v):
                v.tensor_tensor(argb[:], argb[:], Eq[:], ALU.mult)
                v.drain()
                return v.tensor_scalar(qbb[:], argb[:], cols[:, RZQ:RZQ + 1], 0.125,
                                       ALU.mult, ALU.mult)
            qb2 = seq.on("vector", qb1)(qb2f)

            def tq2(t):
                for cc in range(8):
                    last = t.transpose(psT[:, cc * 128:(cc + 1) * 128],
                                       qbb[:, cc * 128:(cc + 1) * 128], c["ident_bf"][:])
                return last
            tq2_ = seq.on("tensor", qb2)(tq2)
            eq2_ = seq.on("vector", tq2_)(lambda v: v.tensor_copy(qbT[:], psT[:]))

            def qstore(sy, ch=ch):
                sy.wait_ge(seq.s["vector"], seq.n["vector"])
                sy.dma_start(out=qrr[:, :, ch * 128:(ch + 1) * 128],
                             in_=v3(qbT[:])).then_inc(s_st, 16)
            block.sync(qstore)
            n_st[0] += 1

            def omm(t):
                for ci in range(8):
                    t.matmul(psA[:, ci * 128:(ci + 1) * 128], Sbf[:, ci * 128:(ci + 1) * 128],
                             v3(qbT[:])[:, ci, :], start=True, stop=True)
                    t.matmul(psP[0:64, ci * 128:(ci + 1) * 128],
                             vt[:, ci * 128:ci * 128 + 64],
                             Pm[:, (2 * ci) * 128:(2 * ci + 1) * 128],
                             start=True, stop=True)
                    last = t.matmul(psP[64:128, ci * 128:(ci + 1) * 128],
                                    vt[:, ci * 128 + 64:(ci + 1) * 128],
                                    Pm[:, (2 * ci + 1) * 128:(2 * ci + 2) * 128],
                                    start=True, stop=True)
                return last
            tO = seq.on("tensor", eq2_)(omm)

            def oevac(v):
                v.wait_ge(s_st, 16 * n_st[0])
                v.tensor_copy(ot_st[:], psA[:])
                v.drain()
                return v.tensor_tensor(ot_st[:], ot_st[:], psP[:, 0:1024], ALU.add)
            oev = seq.on("vector", tO)(oevac)

            def ostore(sy, ch=ch):
                sy.wait_ge(seq.s["vector"], seq.n["vector"])
                sy.dma_start(out=orr[:, :, ch * 128:(ch + 1) * 128],
                             in_=v3(ot_st[:])).then_inc(s_st, 16)
            block.sync(ostore)
            n_st[0] += 1

            def tkc(t):
                for oh in range(2):
                    last = t.matmul(psP[:, 1024 + oh * 512:1024 + (oh + 1) * 512],
                                    c["ones"][0:1, :], brow[0:1, oh * 512:(oh + 1) * 512],
                                    start=True, stop=True)
                return last
            tk3 = seq.on("tensor", oev)(tkc)

            def kcb(v):
                v.tensor_tensor(argb[:], psP[:, 1024:2048], Fc[:], ALU.subtract)
                v.tensor_tensor(cols[:, T2:T2 + 1], cols[:, ZETA:ZETA + 1],
                                cols[:, ZCB:ZCB + 1], ALU.subtract)
                v.drain()
                return v.tensor_scalar(argb[:], argb[:], cols[:, T2:T2 + 1], None,
                                       ALU.add)
            kc1 = seq.on("vector", tk3)(kcb)
            kc2 = seq.on("scalar", kc1)(lambda sc: sc.activation(
                expb[:], argb[:], AF.Exp, bias=0.0, scale=1.0))
            kc3 = seq.on("vector", kc2)(lambda v: v.tensor_tensor(
                ktC[:], expb[:], kkb[:], ALU.mult))

            st_tok = kc3
            for wave in range(2):
                def umm(t, wave=wave):
                    for cj in range(4):
                        ci = wave * 4 + cj
                        last = t.matmul(psM[:, cj * 128:(cj + 1) * 128],
                                        ktC[:, ci * 128:(ci + 1) * 128],
                                        vt[:, ci * 128:(ci + 1) * 128],
                                        start=True, stop=True)
                    return last
                tU = seq.on("tensor", st_tok)(umm)

                def supd(v, wave=wave):
                    for cj in range(4):
                        ci = wave * 4 + cj
                        v.tensor_scalar(S[:, ci * 128:(ci + 1) * 128],
                                        S[:, ci * 128:(ci + 1) * 128],
                                        cols[:, EGCS + ci:EGCS + ci + 1], None, ALU.mult)
                        v.drain()
                        v.tensor_tensor(S[0:64, ci * 128:ci * 128 + 64],
                                        S[0:64, ci * 128:ci * 128 + 64],
                                        psM[0:64, cj * 128:cj * 128 + 64], ALU.add)
                        v.tensor_tensor(S[64:128, ci * 128 + 64:(ci + 1) * 128],
                                        S[64:128, ci * 128 + 64:(ci + 1) * 128],
                                        psM[64:128, cj * 128 + 64:(cj + 1) * 128],
                                        ALU.add)
                        v.drain()
                        last = v.tensor_copy(Sbf[:, ci * 128:(ci + 1) * 128],
                                             S[:, ci * 128:(ci + 1) * 128])
                    return last
                st_tok = seq.on("vector", tU)(supd)
            prev_state[0] = st_tok

    with p.nc.Block() as block:
        @block.sync
        def _(sy):
            sy.dma_start(out=send_dram, in_=S[:]).then_inc(s_st, 16)
            sy.dma_start(out=gtot_dram, in_=Gtot[:]).then_inc(s_st, 16)
            sy.dma_start(out=gch_dram, in_=gch_sb[:]).then_inc(s_st, 16)
            sy.wait_ge(s_st, 16 * (n_st[0] + 3))
    return


def emit_gla_pass2(p, c, oT_dram, qbT_dram, gch_dram, sprev_dram, gprev_dram,
                   wo_dram, x_dram, xout_dram):
    u = p.uid()
    NCH = 16
    wo_sb = p.sb(f"g2_wo_{u}", [128, 8 * 1024], BF16)
    Sin = p.sb(f"g2_Sin_{u}", [128, 1024], F32)
    SinCbf = p.sb(f"g2_sc_{u}", [128, 1024], BF16)
    sprev = p.sb(f"g2_sp_{u}", [128, 3 * 1024], F32)
    gprev = p.sb(f"g2_gp_{u}", [128, 24], F32)
    gch = p.sb(f"g2_gc_{u}", [128, 8 * NCH], F32)
    Dcum = p.sb(f"g2_dc_{u}", [128, 16], F32)
    tmp = p.sb(f"g2_tm_{u}", [128, 1024], F32)
    qch = p.sb(f"g2_qc_{u}", [128, 1024], BF16)
    och = p.sb(f"g2_oc_{u}", [128, 1024], F32)
    obf = p.sb(f"g2_ob_{u}", [128, 1024], BF16)
    xa = p.sb(f"g2_xa_{u}", [128, 1024], F32)
    xo = p.sb(f"g2_xo_{u}", [128, 1024], F32)
    psA = p.ps(f"g2_psA_{u}", [128, 1024], F32)
    psW = p.ps(f"g2_psW_{u}", [128, 1024], F32)
    s_wld = p.sem(f"g2_wld_{u}")
    s_ld = p.sem(f"g2_ld_{u}")
    s_st = p.sem(f"g2_st_{u}")
    n_ld = [0]
    n_st = [0]

    qrr = qbT_dram.rearrange("(ci p) t -> p ci t", p=128)
    orr = oT_dram.rearrange("(ci p) t -> p ci t", p=128)

    with p.nc.Block() as block:
        @block.gpsimd
        def _(g):
            wr = wo_dram.rearrange("(dc p) o -> p dc o", p=128)
            for dc in range(8):
                g.dma_start(out=v3(wo_sb[:])[:, dc, :], in_=wr[:, dc, :]).then_inc(s_wld, 16)

        @block.sync
        def _(sy):
            sy.dma_start(out=sprev[:], in_=sprev_dram).then_inc(s_ld, 16)
            sy.dma_start(out=gprev[:], in_=gprev_dram).then_inc(s_ld, 16)
            sy.dma_start(out=gch[:], in_=gch_dram).then_inc(s_ld, 16)

        @block.vector
        def _(v):
            v.memset(Dcum[:, 0:8], 1.0)

    with p.nc.Block() as block:
        seq = Seq(p, block, f"g2_{u}")

        def dinit(sc):
            sc.wait_ge(s_ld, 48)
            sc.activation(gprev[:, 0:8], gprev[:, 0:8], AF.Exp, bias=0.0, scale=1.0)
            return sc.activation(gprev[:, 8:16], gprev[:, 8:16], AF.Exp, bias=0.0, scale=1.0)
        di = seq.on("scalar")(dinit)

        def sinit(v):
            t8 = tmp[:].rearrange("p (ci d) -> p ci d", ci=8)
            a1, a2 = bass.broadcast_tensor_aps(
                sprev[:, 2048:3072].rearrange("p (ci d) -> p ci d", ci=8),
                gprev[:, 8:16].rearrange("p (ci o) -> p ci o", o=1))
            v.tensor_tensor(t8, a1, a2, ALU.mult)
            v.drain()
            v.tensor_tensor(tmp[:], tmp[:], sprev[:, 1024:2048], ALU.add)
            v.drain()
            a1, a2 = bass.broadcast_tensor_aps(
                t8, gprev[:, 0:8].rearrange("p (ci o) -> p ci o", o=1))
            v.tensor_tensor(t8, a1, a2, ALU.mult)
            v.drain()
            return v.tensor_tensor(Sin[:], tmp[:], sprev[:, 0:1024], ALU.add)
        si = seq.on("vector", di)(sinit)

        prev = si
        for ch in range(NCH):
            def loads(sy, ch=ch):
                sy.wait_ge(seq.s["vector"], seq.n["vector"])
                sy.dma_start(out=qch[:], in_=qrr[:, :, ch * 128:(ch + 1) * 128]
                             ).then_inc(s_ld, 16)
                sy.dma_start(out=och[:], in_=orr[:, :, ch * 128:(ch + 1) * 128]
                             ).then_inc(s_ld, 16)
                sy.dma_start(out=xa[:], in_=x_dram[ch * 128:(ch + 1) * 128, :]
                             ).then_inc(s_ld, 16)
            block.sync(loads)
            n_ld[0] += 3

            def sinc(v):
                a1, a2 = bass.broadcast_tensor_aps(
                    Sin[:].rearrange("p (ci d) -> p ci d", ci=8),
                    Dcum[:, 0:8].rearrange("p (ci o) -> p ci o", o=1))
                v.tensor_tensor(tmp[:].rearrange("p (ci d) -> p ci d", ci=8), a1, a2,
                                ALU.mult)
                v.drain()
                return v.tensor_copy(SinCbf[:], tmp[:])
            sc_tok = seq.on("vector", prev)(sinc)

            n_ld_snap = n_ld[0]

            def corr(t, n_ld_snap=n_ld_snap):
                t.wait_ge(s_wld, 128)
                t.wait_ge(s_ld, 16 * (3 + n_ld_snap))
                for ci in range(8):
                    last = t.matmul(psA[:, ci * 128:(ci + 1) * 128],
                                    SinCbf[:, ci * 128:(ci + 1) * 128],
                                    v3(qch[:])[:, ci, :], start=True, stop=True)
                return last
            ct = seq.on("tensor", sc_tok)(corr)

            def combine(v):
                return v.tensor_tensor(obf[:], psA[:], och[:], ALU.add)
            cb = seq.on("vector", ct)(combine)

            def proj(t):
                for dvh in range(2):
                    for ci in range(8):
                        last = t.matmul(psW[:, dvh * 512:(dvh + 1) * 512],
                                        v3(obf[:])[:, ci, :],
                                        v3(wo_sb[:])[:, ci, dvh * 512:(dvh + 1) * 512],
                                        start=(ci == 0), stop=(ci == 7))
                return last
            pj = seq.on("tensor", cb)(proj)

            n_st_snap = n_st[0]

            def resid(v, n_st_snap=n_st_snap):
                v.wait_ge(s_st, 16 * n_st_snap)
                return v.tensor_tensor(xo[:], psW[:], xa[:], ALU.add)
            rs = seq.on("vector", pj)(resid)

            def store(sy, ch=ch):
                sy.wait_ge(seq.s["vector"], seq.n["vector"])
                sy.dma_start(out=xout_dram[ch * 128:(ch + 1) * 128, :], in_=xo[:]
                             ).then_inc(s_st, 16)
            block.sync(store)
            n_st[0] += 1

            if ch < NCH - 1:
                def dexp(sc, ch=ch):
                    return sc.activation(Dcum[:, 8:16], gch[:, ch * 8:(ch + 1) * 8],
                                         AF.Exp, bias=0.0, scale=1.0)
                de = seq.on("scalar", sc_tok)(dexp)

                def dupd(v):
                    return v.tensor_tensor(Dcum[:, 0:8], Dcum[:, 0:8], Dcum[:, 8:16],
                                           ALU.mult)
                prev = seq.on("vector", de, rs)(dupd)
            else:
                prev = rs

    with p.nc.Block() as block:
        @block.sync
        def _(sy):
            sy.wait_ge(s_st, 16 * NCH)
    return


def emit_router(p, c, xnT, rw_dram, rb_dram, lw_dram, lb_dram, w8_dram, xnT_dram):
    u = p.uid()
    lwf = p.sb(f"rt_lw_{u}", [4, 1024], BF16)
    rwf = p.sb(f"rt_rw_{u}", [8, 1024], BF16)
    lwT = p.sb(f"rt_lwT_{u}", [128, 8 * 4], BF16)
    rwT = p.sb(f"rt_rwT_{u}", [128, 8 * 8], BF16)
    brow = p.sb(f"rt_br_{u}", [1, 12], F32)
    rr = p.sb(f"rt_rr_{u}", [128, 16], F32)
    cols = p.sb(f"rt_co_{u}", [128, 8], F32)
    lp = p.sb(f"rt_lp_{u}", [128, 8], F32)
    ew = p.sb(f"rt_ew_{u}", [128, 8], F32)
    rp = p.sb(f"rt_rp_{u}", [128, 8], F32)
    t8 = p.sb(f"rt_t8_{u}", [128, 8], F32)
    w8 = p.sb(f"rt_w8_{u}", [128, 8], F32)
    psR = p.ps(f"rt_ps_{u}", [128, 512], F32)
    psTt = p.ps(f"rt_pst_{u}", [128, 128], BF16)
    s_ld = p.sem(f"rt_ld_{u}")
    s_st = p.sem(f"rt_st_{u}")
    n_st = [0]

    with p.nc.Block() as block:
        @block.gpsimd
        def _(g):
            g.dma_start(out=lwf[:], in_=lw_dram).then_inc(s_ld, 16)
            g.dma_start(out=rwf[:], in_=rw_dram).then_inc(s_ld, 16)

        @block.sync
        def _(sy):
            sy.dma_start(out=brow[0:1, 0:4],
                         in_=lb_dram.rearrange("(a o) -> a o", a=1)).then_inc(s_ld, 16)
            sy.dma_start(out=brow[0:1, 4:12],
                         in_=rb_dram.rearrange("(a o) -> a o", a=1)).then_inc(s_ld, 16)
            sy.dma_start(out=xnT_dram.rearrange("(dc p) t -> p dc t", p=128),
                         in_=v3(xnT[:])).then_inc(s_st, 16)
    n_st[0] += 1

    with p.nc.Block() as block:
        seq = Seq(p, block, f"rt_{u}")

        def wtr(t):
            t.wait_ge(s_ld, 64)
            for dc in range(8):
                t.transpose(psTt[:, dc * 4:(dc + 1) * 4], lwf[0:4, dc * 128:(dc + 1) * 128],
                            c["ident_bf"][0:4, 0:4])
            for dc in range(8):
                last = t.transpose(psTt[:, 32 + dc * 8:32 + (dc + 1) * 8],
                                   rwf[0:8, dc * 128:(dc + 1) * 128], c["ident_bf"][0:8, 0:8])
            return last
        tw = seq.on("tensor")(wtr)
        cw = seq.on("vector", tw)(lambda v: (
            v.tensor_copy(lwT[:], psTt[:, 0:32]),
            v.tensor_copy(rwT[:], psTt[:, 32:96]))[-1])

        prev = cw
        for tt in range(NT):
            def mms(t, tt=tt):
                for dc in range(8):
                    t.matmul(psR[:, 0:4], v3(xnT[:])[:, dc, tt * 128:(tt + 1) * 128],
                             v3(lwT[:], dc=8)[:, dc, :], start=(dc == 0), stop=False)
                t.matmul(psR[:, 0:4], c["ones"][0:1, 0:128], brow[0:1, 0:4],
                         start=False, stop=True)
                for dc in range(8):
                    t.matmul(psR[:, 4:12], v3(xnT[:])[:, dc, tt * 128:(tt + 1) * 128],
                             v3(rwT[:], dc=8)[:, dc, :], start=(dc == 0), stop=False)
                return t.matmul(psR[:, 4:12], c["ones"][0:1, 0:128], brow[0:1, 4:12],
                                start=False, stop=True)
            tm = seq.on("tensor", prev)(mms)
            ev = seq.on("vector", tm)(lambda v: v.tensor_copy(rr[:], psR[:, 0:16]))

            def smax4(v):
                v.tensor_reduce(cols[:, 0:1], rr[:, 0:4], AXX, ALU.max, negate=True)
                return v.drain() or v.tensor_copy(cols[:, 7:8], cols[:, 0:1])
            sm4 = seq.on("vector", ev)(lambda v: (
                v.tensor_reduce(cols[:, 0:1], rr[:, 0:4], AXX, ALU.max, negate=True))
                )
            e4 = seq.on("scalar", sm4)(lambda s: s.activation(
                lp[:, 0:4], rr[:, 0:4], AF.Exp, bias=cols[:, 0:1], scale=1.0,
                accum_out=cols[:, 1:2]))

            def lpn(v):
                v.reciprocal(cols[:, 2:3], cols[:, 1:2])
                v.drain()
                v.tensor_scalar(lp[:, 0:4], lp[:, 0:4], cols[:, 2:3], None, ALU.mult)
                v.drain()
                v.tensor_tensor(ew[:, 0:3], lp[:, 0:3], lp[:, 1:4], ALU.add)
                v.tensor_copy(ew[:, 3:4], lp[:, 3:4])
                v.drain()
                v.tensor_copy(ew[:, 4:8], ew[:, 0:4])
                v.drain()
                v.scalar_tensor_tensor(rp[:], ew[:], 0.1, rr[:, 4:12], ALU.mult, ALU.add)
                v.drain()
                return v.tensor_reduce(cols[:, 3:4], rp[:], AXX, ALU.max, negate=True)
            ln_ = seq.on("vector", e4)(lpn)
            e8 = seq.on("scalar", ln_)(lambda s: s.activation(
                rp[:], rp[:], AF.Exp, bias=cols[:, 3:4], scale=1.0,
                accum_out=cols[:, 4:5]))

            n_st_snap = n_st[0]

            def top2(v, n_st_snap=n_st_snap):
                v.reciprocal(cols[:, 5:6], cols[:, 4:5])
                v.drain()
                v.tensor_scalar(rp[:], rp[:], cols[:, 5:6], None, ALU.mult)
                v.drain()
                v.max(t8[:], rp[:])
                v.drain()
                v.tensor_tensor(cols[:, 6:7], t8[:, 0:1], t8[:, 1:2], ALU.add)
                v.drain()
                v.reciprocal(cols[:, 6:7], cols[:, 6:7])
                v.tensor_scalar(w8[:], rp[:], t8[:, 1:2], None, ALU.is_ge)
                v.drain()
                v.wait_ge(s_st, 16 * n_st_snap)
                v.tensor_tensor(w8[:], w8[:], rp[:], ALU.mult)
                v.drain()
                return v.tensor_scalar(w8[:], w8[:], cols[:, 6:7], None, ALU.mult)
            tp2 = seq.on("vector", e8)(top2)

            def wst(sy, tt=tt):
                sy.wait_ge(seq.s["vector"], seq.n["vector"])
                sy.dma_start(out=w8_dram[tt * 128:(tt + 1) * 128, :], in_=w8[:]
                             ).then_inc(s_st, 16)
            block.sync(wst)
            n_st[0] += 1
            prev = tp2

    with p.nc.Block() as block:
        @block.sync
        def _(sy):
            sy.wait_ge(s_st, 16 * n_st[0])
    return


def emit_expert(p, c, xnT_dram, wcol_dram, w1_dram, b1_dram, w2_dram, b2_dram,
                part_dram, TTOT=16384):
    u = p.uid()
    NB = TTOT // 512
    w1sb = p.sb(f"ex_w1_{u}", [128, 8 * 4096], BF16)
    w2sb = p.sb(f"ex_w2_{u}", [128, 32 * 1024], BF16)
    b1c = p.sb(f"ex_b1_{u}", [128, 32], F32)
    b2row = p.sb(f"ex_b2_{u}", [1, 1024], F32)
    wcols = p.sb(f"ex_wc_{u}", [128, TTOT // 128], F32)
    xnb = [p.sb(f"ex_xn{j}_{u}", [128, 8 * 512], BF16) for j in range(2)]
    h_sb = p.sb(f"ex_h_{u}", [128, 32 * 512], BF16)
    part = p.sb(f"ex_pt_{u}", [128, 2 * 1024], F32)
    psH = p.ps(f"ex_psH_{u}", [128, 1024], F32)
    psE = p.ps(f"ex_psE_{u}", [128, 1024], F32)
    s_wld = p.sem(f"ex_wld_{u}")
    s_bld = p.sem(f"ex_bld_{u}")
    s_x = [p.sem(f"ex_x{j}_{u}") for j in range(2)]
    s_h = p.sem(f"ex_h_{u}")
    s_he = p.sem(f"ex_he_{u}")
    s_e = p.sem(f"ex_e_{u}")
    s_ev = p.sem(f"ex_ev_{u}")
    s_st = p.sem(f"ex_st_{u}")

    emit_bias_cols(p, c, b1_dram.rearrange("(a o) -> a o", a=1), b1c, 32)
    xrr = xnT_dram.rearrange("(dc p) t -> p dc t", p=128)

    with p.nc.Block() as block:
        @block.gpsimd
        def _(g):
            wr1 = w1_dram.rearrange("(dc p) o -> p dc o", p=128)
            for dc in range(8):
                g.dma_start(out=v3(w1sb[:])[:, dc, :], in_=wr1[:, dc, :]).then_inc(s_wld, 16)
            wr2 = w2_dram.rearrange("(dc p) o -> p dc o", p=128)
            for dc in range(32):
                g.dma_start(out=v3(w1sb[:], dc=8)[:, 0, 0:1] if False else
                            w2sb[:].rearrange("p (dc o) -> p dc o", dc=32)[:, dc, :],
                            in_=wr2[:, dc, :]).then_inc(s_wld, 16)

        @block.sync
        def _(sy):
            sy.dma_start(out=b2row[:], in_=b2_dram.rearrange("(a o) -> a o", a=1)
                         ).then_inc(s_bld, 16)
            sy.dma_start(out=wcols[:], in_=wcol_dram).then_inc(s_bld, 16)

    with p.nc.Block() as block:
        @block.sync
        def _(sy):
            def store(ti):
                pb = (ti % 2) * 1024
                sy.wait_ge(s_ev, 2 * ti + 2)
                sy.dma_start(out=part_dram[ti * 128:(ti + 1) * 128, :],
                             in_=part[:, pb:pb + 1024]).then_inc(s_st, 16)
            for tb in range(NB):
                if tb >= 2:
                    sy.wait_ge(s_h, 32 * (tb - 1))
                sy.dma_start(out=v3(xnb[tb % 2][:])[:, :, :],
                             in_=xrr[:, :, tb * 512:(tb + 1) * 512]).then_inc(s_x[tb % 2], 16)
                if tb >= 1:
                    for ti in range(4 * (tb - 1), 4 * tb):
                        store(ti)

        @block.tensor
        def _(t):
            t.wait_ge(s_wld, 16 * 40)
            t.wait_ge(s_bld, 32)
            for tb in range(NB):
                t.wait_ge(s_x[tb % 2], 16 * (tb // 2 + 1))
                for os in range(32):
                    if tb * 32 + os >= 2:
                        t.wait_ge(s_he, tb * 32 + os - 1)
                    hb = (os % 2) * 512
                    for dc in range(8):
                        last = t.matmul(psH[:, hb:hb + 512],
                                        v3(w1sb[:])[:, dc, os * 128:(os + 1) * 128],
                                        v3(xnb[tb % 2][:])[:, dc, :],
                                        start=(dc == 0), stop=(dc == 7))
                    last.then_inc(s_h, 1)
                for tl in range(4):
                    t.wait_ge(s_he, tb * 32 + 32)
                    if tb * 4 + tl >= 1:
                        t.wait_ge(s_ev, 2 * (tb * 4 + tl))
                    for k2s in range(32):
                        for dvh in range(2):
                            t.matmul(psE[:, dvh * 512:(dvh + 1) * 512],
                                     v3(h_sb[:], dc=32)[:, k2s, tl * 128:(tl + 1) * 128],
                                     w2sb[:].rearrange("p (dc o) -> p dc o", dc=32)[:, k2s,
                                                                                    dvh * 512:(dvh + 1) * 512],
                                     start=(k2s == 0), stop=False)
                    for dvh in range(2):
                        last = t.matmul(psE[:, dvh * 512:(dvh + 1) * 512],
                                        c["ones"][0:1, 0:128],
                                        b2row[0:1, dvh * 512:(dvh + 1) * 512],
                                        start=False, stop=True)
                    last.then_inc(s_e, 1)

        @block.scalar
        def _(s):
            for tb in range(NB):
                for os in range(32):
                    s.wait_ge(s_h, tb * 32 + os + 1)
                    s.activation(v3(h_sb[:], dc=32)[:, os, :], psH[:, (os % 2) * 512:(os % 2) * 512 + 512],
                                 AF.Gelu, bias=b1c[:, os:os + 1], scale=1.0).then_inc(s_he, 1)
                for tl in range(4):
                    ti = tb * 4 + tl
                    s.wait_ge(s_e, ti + 1)
                    if ti >= 2:
                        s.wait_ge(s_st, 16 * (ti - 1))
                    pb = (ti % 2) * 1024
                    s.activation(part[:, pb:pb + 512], psE[:, 0:512], AF.Identity,
                                 bias=0.0, scale=wcols[:, ti:ti + 1])
                    s.activation(part[:, pb + 512:pb + 1024], psE[:, 512:1024], AF.Identity,
                                 bias=0.0, scale=wcols[:, ti:ti + 1]).then_inc(s_ev, 2)

        @block.sync
        def _(sy):
            for ti in range(4 * (NB - 1), 4 * NB):
                pb = (ti % 2) * 1024
                sy.wait_ge(s_ev, 2 * ti + 2)
                sy.dma_start(out=part_dram[ti * 128:(ti + 1) * 128, :],
                             in_=part[:, pb:pb + 1024]).then_inc(s_st, 16)
            sy.wait_ge(s_st, 16 * NB * 4)
    return


def emit_combine(p, c, x_dram, parts_dram, xout_dram):
    u = p.uid()
    xa = p.sb(f"cb_xa_{u}", [128, 1024], F32)
    pa = p.sb(f"cb_pa_{u}", [128, 8 * 1024], F32)
    acc = p.sb(f"cb_ac_{u}", [128, 2 * 1024], F32)
    s_ld = p.sem(f"cb_ld_{u}")
    s_ac = p.sem(f"cb_ac_{u}")
    s_st = p.sem(f"cb_st_{u}")

    with p.nc.Block() as block:
        @block.sync
        def _(sy):
            def store(tt):
                b = (tt % 2) * 1024
                sy.wait_ge(s_ac, tt + 1)
                sy.dma_start(out=xout_dram[tt * 128:(tt + 1) * 128, :],
                             in_=acc[:, b:b + 1024]).then_inc(s_st, 16)
            for tt in range(NT):
                if tt >= 1:
                    sy.wait_ge(s_ac, tt)
                sy.dma_start(out=xa[:], in_=x_dram[tt * 128:(tt + 1) * 128, :]
                             ).then_inc(s_ld, 16)
                for j in range(8):
                    sy.dma_start(out=pa[:, j * 1024:(j + 1) * 1024],
                                 in_=parts_dram[j, tt * 128:(tt + 1) * 128, :]
                                 ).then_inc(s_ld, 16)
                if tt >= 1:
                    store(tt - 1)
            store(NT - 1)
            sy.wait_ge(s_st, 16 * NT)

        @block.vector
        def _(v):
            for tt in range(NT):
                b = (tt % 2) * 1024
                v.wait_ge(s_ld, 16 * 9 * (tt + 1))
                if tt >= 2:
                    v.wait_ge(s_st, 16 * (tt - 1))
                for j in range(8):
                    src = xa[:] if j == 0 else acc[:, b:b + 1024]
                    last = v.tensor_tensor(acc[:, b:b + 1024], src,
                                           pa[:, j * 1024:(j + 1) * 1024], ALU.add)
                    if j < 7:
                        v.drain()
                last.then_inc(s_ac, 1)
    return


def _build_programs():
    progs = {}

    pA = Prog("A_qkv")
    xA = pA.din("x_in", [T, D])
    wA = pA.din("w_in", [1024, 3072])
    bA = pA.din("b_in", [3072])
    qT_o = pA.dout("qT_o", [128, 8 * T], BF16)
    kT_o = pA.dout("kT_o", [128, 8 * T], BF16)
    v_o = pA.dout("v_o", [128, 16 * 1024], BF16)
    qss_o = pA.dout("qss_o", [1, T])
    kmax_o = pA.dout("kmax_o", [1, 1])
    cA = emit_consts(pA, ccvals=[EPS, -0.5])
    xnT = pA.sb("xnT", [128, 8 * T], BF16)
    qT = pA.sb("qT", [128, 8 * T], BF16)
    kT = pA.sb("kT", [128, 8 * T], BF16)
    v_sb = pA.sb("v_sb", [128, 16 * 1024], BF16)
    qss = pA.sb("qss", [1, T])
    kmax = pA.sb("kmax", [1, 1])
    pA.begin(); emit_ln_xnT(pA, cA, xA, xnT); pA.end()
    pA.begin(); emit_qkv(pA, cA, xnT, wA, bA, qT, kT, v_sb, qss, kmax); pA.end()
    s_st = pA.sem("st")
    with pA.nc.Block() as block:
        @block.sync
        def _(sync):
            n = 0
            for dst, src in [(qT_o, qT), (kT_o, kT), (v_o, v_sb), (qss_o, qss),
                             (kmax_o, kmax)]:
                sync.dma_start(out=dst, in_=src[:]).then_inc(s_st, 16)
                n += 1
            sync.wait_ge(s_st, 16 * n)
    pA.close()
    progs["A"] = pA

    pB = Prog("B_attn")
    qT_i = pB.din("qT_i", [128, 8 * T], BF16)
    kT_i = pB.din("kT_i", [1024, 8192], BF16)
    v_i = pB.din("v_i", [8192, 1024], BF16)
    qss_i = pB.din("qss_i", [1, T])
    kss4_i = pB.din("kss4_i", [1, 4])
    gam_i = pB.din("gam_i", [1, 1])
    wp_i = pB.din("wp_i", [1024, 1024])
    bp_i = pB.din("bp_i", [1024])
    x_i = pB.din("x_i", [T, D])
    xo_o = pB.dout("xo_o", [T, D])
    cB = emit_consts(pB, ccvals=[EPS, -0.5, 0.03125])
    qTs = pB.sb("qTs", [128, 8 * T], BF16)
    s_q = pB.sem("sq")
    with pB.nc.Block() as block:
        @block.sync
        def _(sync):
            sync.dma_start(out=qTs[:], in_=qT_i).then_inc(s_q, 16)
            sync.wait_ge(s_q, 16)
    pB.begin()
    emit_attn(pB, cB, 0, qTs, kT_i, v_i, qss_i, kss4_i, gam_i, wp_i, bp_i, x_i, xo_o)
    pB.end()
    pB.close()
    progs["B"] = pB

    pC = Prog("C_ffn0")
    xC = pC.din("x_i", [T, D])
    winC = pC.din("win_i", [1024, 64]); binC = pC.din("bin_i", [64])
    w1C = pC.din("w1_i", [64, 16]); b1C = pC.din("b1_i", [64, 16])
    w2C = pC.din("w2_i", [64, 16]); b2C = pC.din("b2_i", [64])
    woC = pC.din("wout_i", [64, 1024]); boC = pC.din("bout_i", [1024])
    xoC = pC.dout("xo_o", [T, D])
    cC = emit_consts(pC, ccvals=[EPS, -0.5])
    xnTC = pC.sb("xnT", [128, 8 * T], BF16)
    pC.begin(); emit_ln_xnT(pC, cC, xC, xnTC); pC.end()
    pC.begin()
    emit_ffn0(pC, cC, xnTC, winC, binC, w1C, b1C, w2C, b2C, woC, boC, xC, xoC)
    pC.end()
    pC.close()
    progs["C"] = pC

    pD = Prog("D_gla1")
    xD = pD.din("x_in", [T, D])
    wqD = pD.din("wq_in", [1024, 1024]); wfD = pD.din("wf_in", [1024, 1024])
    wiD = pD.din("wi_in", [1024, 1024])
    mAD = pD.din("mA_in", [128, 128], BF16); mA2D = pD.din("mA2_in", [128, 128], BF16)
    mBD = pD.din("mB_in", [128, 128], BF16); mCD = pD.din("mC_in", [128, 128], BF16)
    s8D = pD.din("sel8_in", [128, 16]); s8pD = pD.din("sel8p_in", [128, 16])
    e8tD = pD.din("e8t_in", [16, 128])
    oTD = pD.dout("oT_o", [1024, T]); qbTD = pD.dout("qbT_o", [1024, T], BF16)
    seD = pD.dout("send_o", [128, 1024]); gtD = pD.dout("gtot_o", [128, 8])
    gcD = pD.dout("gch_o", [128, 128])
    cD = emit_consts(pD, ccvals=[EPS, -0.5])
    xnTD = pD.sb("xnT", [128, 8 * T], BF16)
    pD.begin(); emit_ln_xnT(pD, cD, xD, xnTD); pD.end()
    pD.begin()
    emit_gla_pass1(pD, cD, xnTD, wqD, wfD, wiD, oTD, qbTD, seD, gtD, gcD,
                   mAD, mA2D, mBD, mCD, s8D, s8pD, e8tD)
    pD.end()
    pD.close()
    progs["D"] = pD

    pE = Prog("E_gla2")
    oTE = pE.din("oT_i", [1024, T]); qbTE = pE.din("qbT_i", [1024, T], BF16)
    gchE = pE.din("gch_i", [128, 128])
    spE = pE.din("sprev_i", [128, 3 * 1024]); gpE = pE.din("gprev_i", [128, 24])
    woE = pE.din("wo_in", [1024, 1024])
    xE = pE.din("x_in", [T, D])
    xoE = pE.dout("xo_o", [T, D])
    cE = emit_consts(pE, ccvals=[EPS, -0.5])
    pE.begin()
    emit_gla_pass2(pE, cE, oTE, qbTE, gchE, spE, gpE, woE, xE, xoE)
    pE.end()
    pE.close()
    progs["E"] = pE

    pF = Prog("F_router")
    xF = pF.din("x_in", [T, D])
    rwF = pF.din("rw_in", [8, 1024]); rbF = pF.din("rb_in", [8])
    lwF = pF.din("lw_in", [4, 1024]); lbF = pF.din("lb_in", [4])
    w8F = pF.dout("w8_o", [T, 8])
    xnTF = pF.dout("xnT_o", [1024, T], BF16)
    cF = emit_consts(pF, ccvals=[EPS, -0.5])
    xnTFs = pF.sb("xnT", [128, 8 * T], BF16)
    pF.begin(); emit_ln_xnT(pF, cF, xF, xnTFs); pF.end()
    pF.begin(); emit_router(pF, cF, xnTFs, rwF, rbF, lwF, lbF, w8F, xnTF); pF.end()
    pF.close()
    progs["F"] = pF

    pG = Prog("G_expert")
    xnTG = pG.din("xnT_in", [1024, 16384], BF16)
    wcG = pG.din("wcol_in", [128, 128])
    w1G = pG.din("w1_in", [1024, 4096]); b1G = pG.din("b1_in", [4096])
    w2G = pG.din("w2_in", [4096, 1024]); b2G = pG.din("b2_in", [1024])
    ptG = pG.dout("part_o", [16384, 1024])
    cG = emit_consts(pG, ccvals=[EPS])
    pG.begin(); emit_expert(pG, cG, xnTG, wcG, w1G, b1G, w2G, b2G, ptG); pG.end()
    pG.close()
    progs["G"] = pG

    pH = Prog("H_combine")
    xH = pH.din("x_in", [T, D])
    paH = pH.din("parts_in", [8, T, 1024])
    xoH = pH.dout("xo_o", [T, D])
    cH = emit_consts(pH, ccvals=[EPS])
    pH.begin(); emit_combine(pH, cH, xH, paH, xoH); pH.end()
    pH.close()
    progs["H"] = pH
    return progs


_PROGS = None


def kernel(**inputs):
    global _PROGS
    if _PROGS is None:
        _PROGS = _build_programs()
    P = _PROGS
    cores = list(range(NC))
    f32 = np.float32

    x = np.ascontiguousarray(inputs["L0_attn_x"], dtype=f32)
    gamma = np.asarray(inputs["L0_attn_gamma"], dtype=f32).reshape(1, 1)
    shards = [np.ascontiguousarray(x[c // 4, (c % 4) * T:(c % 4 + 1) * T])
              for c in range(NC)]

    Wqkv = np.ascontiguousarray(inputs["L0_attn_Wqkv"], dtype=f32)
    bqkv = np.ascontiguousarray(inputs["L0_attn_bqkv"], dtype=f32)
    inA = [dict(x_in=shards[c], w_in=Wqkv, b_in=bqkv) for c in range(NC)]
    rA = run_bass_kernel_spmd(P["A"].nc, inA, core_ids=cores).results

    Wp = np.ascontiguousarray(inputs["L0_attn_Wproj"], dtype=f32)
    bp = np.ascontiguousarray(inputs["L0_attn_bproj"], dtype=f32)
    inB = []
    kT_full, v_full, kmaxs = {}, {}, {}
    for b in range(2):
        ks = [rA[4 * b + j]["kT_o"].reshape(128, 8, T) for j in range(4)]
        kT_full[b] = np.concatenate(ks, axis=2)
        vs = [rA[4 * b + j]["v_o"].reshape(128, 16, 1024) for j in range(4)]
        v_full[b] = np.concatenate(vs, axis=1)
        kmaxs[b] = np.stack([rA[4 * b + j]["kmax_o"][0, 0] for j in range(4)])
    for c in range(NC):
        b = c // 4
        roll = (c % 4) * 16
        kf = kT_full[b].reshape(128, 8, 64, 128)
        kf = np.roll(kf, -roll, axis=2).reshape(128, 8, 8192)
        kT_dev = np.ascontiguousarray(kf.transpose(1, 0, 2).reshape(1024, 8192))
        vf = np.roll(v_full[b], -roll, axis=1)
        v_dev = np.ascontiguousarray(vf.transpose(1, 0, 2).reshape(8192, 1024))
        inB.append(dict(qT_i=rA[c]["qT_o"], kT_i=kT_dev, v_i=v_dev,
                        qss_i=rA[c]["qss_o"],
                        kss4_i=np.ascontiguousarray(kmaxs[b][None, :], dtype=f32),
                        gam_i=gamma, wp_i=Wp, bp_i=bp, x_i=shards[c]))
    rB = run_bass_kernel_spmd(P["B"].nc, inB, core_ids=cores).results
    x1 = [rB[c]["xo_o"] for c in range(NC)]

    inC = [dict(x_i=x1[c], win_i=inputs["L0_ffn_Win"], bin_i=inputs["L0_ffn_bin_"],
                w1_i=inputs["L0_ffn_W1"], b1_i=inputs["L0_ffn_b1"],
                w2_i=inputs["L0_ffn_W2"], b2_i=inputs["L0_ffn_b2"],
                wout_i=inputs["L0_ffn_Wout"], bout_i=inputs["L0_ffn_bout"])
           for c in range(NC)]
    inC = [{k: np.ascontiguousarray(v, dtype=f32) for k, v in d.items()} for d in inC]
    rC = run_bass_kernel_spmd(P["C"].nc, inC, core_ids=cores).results
    x2 = [rC[c]["xo_o"] for c in range(NC)]

    si = np.arange(128)
    s4 = (si // 4)[:, None]; t4 = (si // 4)[None, :]
    s8 = (si // 8)[:, None]; t8 = (si // 8)[None, :]
    sle = si[:, None] <= si[None, :]
    bf = ml_dtypes.bfloat16
    mA = ((t4 == s4) & sle).astype(bf)
    mA2 = (t4 == s4 + 1).astype(bf)
    mB = ((t8 == s8 + 1) & (t4 >= s4 + 2)).astype(bf)
    mC = (t8 >= s8 + 2).astype(bf)
    sel8 = (si[:, None] == 8 * np.arange(16)[None, :] + 7).astype(f32)
    sel8p = (si[:, None] == 8 * np.arange(16)[None, :] - 1).astype(f32)
    e8t = (np.arange(16)[:, None] == si[None, :] // 8).astype(f32)
    inD = [dict(x_in=x2[c], wq_in=inputs["L1_attn_Wq"], wf_in=inputs["L1_attn_Wf"],
                wi_in=inputs["L1_attn_Wi"]) for c in range(NC)]
    inD = [{k: np.ascontiguousarray(v, dtype=f32) for k, v in d.items()} for d in inD]
    for d in inD:
        d.update(mA_in=mA, mA2_in=mA2, mB_in=mB, mC_in=mC,
                 sel8_in=sel8, sel8p_in=sel8p, e8t_in=e8t)
    rD = run_bass_kernel_spmd(P["D"].nc, inD, core_ids=cores).results

    Wo = np.ascontiguousarray(inputs["L1_attn_Wo"], dtype=f32)
    inE = []
    for c in range(NC):
        sprev = np.zeros((128, 3 * 1024), dtype=f32)
        gprev = np.zeros((128, 24), dtype=f32)
        for j in range(3):
            src = c - 1 - j
            if src >= 0 and src // 4 == c // 4:
                sprev[:, j * 1024:(j + 1) * 1024] = rD[src]["send_o"]
                gprev[:, j * 8:(j + 1) * 8] = rD[src]["gtot_o"]
        inE.append(dict(oT_i=rD[c]["oT_o"], qbT_i=rD[c]["qbT_o"], gch_i=rD[c]["gch_o"],
                        sprev_i=sprev, gprev_i=gprev, wo_in=Wo, x_in=x2[c]))
    rE = run_bass_kernel_spmd(P["E"].nc, inE, core_ids=cores).results
    x3 = [rE[c]["xo_o"] for c in range(NC)]

    inF = [dict(x_in=x3[c], rw_in=inputs["L1_ffn_router_w"], rb_in=inputs["L1_ffn_router_b"],
                lw_in=inputs["L1_ffn_lang_w"], lb_in=inputs["L1_ffn_lang_b"])
           for c in range(NC)]
    inF = [{k: np.ascontiguousarray(v, dtype=f32) for k, v in d.items()} for d in inF]
    rF = run_bass_kernel_spmd(P["F"].nc, inF, core_ids=cores).results
    xnT_full = np.concatenate([rF[c]["xnT_o"] for c in range(NC)], axis=1)
    w8_full = np.concatenate([rF[c]["w8_o"] for c in range(NC)], axis=0)

    w1 = np.ascontiguousarray(inputs["L1_ffn_w1"], dtype=f32)
    b1 = np.ascontiguousarray(inputs["L1_ffn_b1"], dtype=f32)
    w2 = np.ascontiguousarray(inputs["L1_ffn_w2"], dtype=f32)
    b2 = np.ascontiguousarray(inputs["L1_ffn_b2"], dtype=f32)
    inG = [dict(xnT_in=xnT_full,
                wcol_in=np.ascontiguousarray(w8_full[:, j].reshape(128, 128).T),
                w1_in=w1[j], b1_in=b1[j], w2_in=w2[j], b2_in=b2[j]) for j in range(NC)]
    rG = run_bass_kernel_spmd(P["G"].nc, inG, core_ids=cores).results

    inH = []
    for c in range(NC):
        parts = np.stack([rG[j]["part_o"][c * T:(c + 1) * T] for j in range(NC)])
        inH.append(dict(x_in=x3[c], parts_in=parts))
    rH = run_bass_kernel_spmd(P["H"].nc, inH, core_ids=cores).results

    out = np.stack([np.concatenate([rH[4 * b + j]["xo_o"] for j in range(4)], axis=0)
                    for b in range(2)])
    return out
